# version 15
# baseline (speedup 1.0000x reference)
import numpy as np
from contextlib import ExitStack

import concourse.bass as bass
import concourse.mybir as mybir
from concourse.bass_utils import run_bass_kernel_spmd

F32 = mybir.dt.float32
BF16 = mybir.dt.bfloat16
I32 = mybir.dt.int32
ALU = mybir.AluOpType
AF = mybir.ActivationFunctionType
AX = mybir.AxisListType

T = 2048
ATT = 2176
X1R = 2304
PRE = 14208
DIM = 1024
DH = 4096
EPS = 1e-5
SOFTB = 30.0
SCALE = 0.125
CSCALE = 0.125 / 64.0
VSCALE = 0.5 / 64.0
ABLK = [(0, 128), (128, 512), (640, 512), (1152, 512), (1664, 512)]
FBLK = 256
NFB = T // FBLK


class Ctr:
    def __init__(self, sem, step=1):
        self.sem, self.step, self.n = sem, step, 0

    def tag(self, inst):
        inst.then_inc(self.sem, self.step)
        self.n += self.step
        return self.n

    def tagw(self, eng, inst):
        return self.tag(inst)


def build_nc():
    nc = bass.Bass("TRN2", target_bir_lowering=False, debug=False,
                   detect_race_conditions=False)

    xext = nc.dram_tensor("xext", [ATT, DIM], F32, kind="ExternalInput").ap()
    xr1 = nc.dram_tensor("xr1", [1, DIM], F32, kind="ExternalInput").ap()
    xpre = nc.dram_tensor("xpre", [PRE, DIM], F32, kind="ExternalInput").ap()
    npre = nc.dram_tensor("npre", [1, 1], F32, kind="ExternalInput").ap()
    haslt = nc.dram_tensor("haslt", [1, 1], F32, kind="ExternalInput").ap()
    hasrt = nc.dram_tensor("hasrt", [1, 1], F32, kind="ExternalInput").ap()
    wqkv = nc.dram_tensor("wqkv", [DIM, 3 * DIM], F32, kind="ExternalInput").ap()
    woutd = nc.dram_tensor("woutd", [DIM, DIM], F32, kind="ExternalInput").ap()
    w1 = nc.dram_tensor("w1", [DH, DIM], F32, kind="ExternalInput").ap()
    w2 = nc.dram_tensor("w2", [DIM, DH], F32, kind="ExternalInput").ap()
    cvec = nc.dram_tensor("cvec", [1, DIM], F32, kind="ExternalInput").ap()
    wmod = nc.dram_tensor("wmod", [DIM, 6 * DIM], F32, kind="ExternalInput").ap()
    convw = nc.dram_tensor("convw", [128, 96], F32, kind="ExternalInput").ap()
    convb = nc.dram_tensor("convb", [128, 32], F32, kind="ExternalInput").ap()
    outd = nc.dram_tensor("outd", [T, DIM], F32, kind="ExternalOutput").ap()
    x1d = nc.dram_tensor("x1d", [X1R, DIM], F32, kind="Internal").ap()
    t1d = nc.dram_tensor("t1d", [DH, DIM], BF16, kind="Internal").ap()
    t2d = nc.dram_tensor("t2d", [DIM, DH], BF16, kind="Internal").ap()
    kqd = nc.dram_tensor("kqd", [8 * 128, X1R], BF16, kind="Internal").ap()
    yd = nc.dram_tensor("yd", [NFB * 32, 128 * FBLK], F32, kind="Internal").ap()
    m1d = nc.dram_tensor("m1d", [1, X1R], F32, kind="Internal").ap()

    ctx = ExitStack()
    with ctx:
        e = ctx.enter_context

        LD = Ctr(e(nc.semaphore("sLD")), 16)
        SG = Ctr(e(nc.semaphore("sSG")), 16)
        TC = Ctr(e(nc.semaphore("sTC")))
        VC = Ctr(e(nc.semaphore("sVC")))
        AC = Ctr(e(nc.semaphore("sAC")))
        GC = Ctr(e(nc.semaphore("sGC")))

        ident = e(nc.sbuf_tensor("ident", [128, 128], F32))
        ident1 = e(nc.sbuf_tensor("ident1", [1, 1], F32))
        cst_a = e(nc.sbuf_tensor("cst_a", [128, 1], F32))
        cst_b = e(nc.sbuf_tensor("cst_b", [128, 1], F32))
        cst_c = e(nc.sbuf_tensor("cst_c", [128, 1], F32))
        cst_d = e(nc.sbuf_tensor("cst_d", [128, 1], F32))
        identb = e(nc.sbuf_tensor("identb", [128, 128], BF16))
        ones_c = e(nc.sbuf_tensor("ones_c", [128, 1], F32))
        ones_r = e(nc.sbuf_tensor("ones_r", [1, 128], F32))
        onesE = e(nc.sbuf_tensor("onesE", [64, 33], F32))
        onesO = e(nc.sbuf_tensor("onesO", [64, 33], F32))
        ones11 = e(nc.sbuf_tensor("ones11", [1, 11], F32))
        onesVs = [e(nc.sbuf_tensor(f"onesVs{c}", [64, 10], F32)) for c in range(10)]
        halfsel = e(nc.sbuf_tensor("halfsel", [33, 128], F32))
        parmask = e(nc.sbuf_tensor("parmask", [128, 2], F32))
        tri = e(nc.sbuf_tensor("tri", [10, 11], F32))
        tmp_row = e(nc.sbuf_tensor("tmp_row", [1, DIM], F32))
        sc_cols = e(nc.sbuf_tensor("sc_cols", [128, 8], F32))
        ca1_cols = e(nc.sbuf_tensor("ca1_cols", [128, 8], F32))
        sa_cols = e(nc.sbuf_tensor("sa_cols", [128, 8], F32))
        cb1_rep = e(nc.sbuf_tensor("cb1_rep", [128, DIM], F32))
        sb_rep = e(nc.sbuf_tensor("sb_rep", [128, DIM], F32))
        gb_rep = e(nc.sbuf_tensor("gb_rep", [128, DIM], F32))
        ga_row = e(nc.sbuf_tensor("ga_row", [1, DIM], F32))
        flags = e(nc.sbuf_tensor("flags", [1, 4], F32))
        haslc = e(nc.sbuf_tensor("haslc", [128, 1], F32))
        hasrc = e(nc.sbuf_tensor("hasrc", [128, 1], F32))
        cw_sb = e(nc.sbuf_tensor("cw_sb", [128, 96], F32))
        cb_sb = e(nc.sbuf_tensor("cb_sb", [128, 32], F32))
        wsc = e(nc.sbuf_tensor("wsc", [1, 8], F32))
        wsc_col = e(nc.sbuf_tensor("wsc_col", [128, 2], F32))
        m_col = e(nc.sbuf_tensor("m_col", [128, 2], F32))
        m127 = e(nc.sbuf_tensor("m127", [128, 2], F32))
        hmodT = e(nc.sbuf_tensor("hmodT", [128, 8], F32))
        basek = e(nc.sbuf_tensor("basek", [128, 8], F32))
        carryk = e(nc.sbuf_tensor("carryk", [128, 8], F32))
        hr_cols = e(nc.sbuf_tensor("hr_cols", [128, 8], F32))
        qr_sb = e(nc.sbuf_tensor("qr_sb", [128, 8], F32))
        vr_sb = e(nc.sbuf_tensor("vr_sb", [128, 8], F32))
        xr_row = e(nc.sbuf_tensor("xr_row", [1, DIM], F32))

        ps = [e(nc.psum_tensor(f"ps{i}", [128, 512], F32)) for i in range(8)]

        nc.const_aps.register(F32, -SOFTB, cst_a[:])
        nc.const_aps.register(F32, SCALE, cst_b[:])
        nc.const_aps.register(F32, CSCALE, cst_c[:])
        nc.const_aps.register(F32, EPS, cst_d[:])

        block = e(nc.Block())

        p01 = ExitStack()
        c_row = p01.enter_context(nc.sbuf_tensor("c_row", [1, DIM], F32))
        mod_row = p01.enter_context(nc.sbuf_tensor("mod_row", [1, 6 * DIM], F32))
        hsum_row = p01.enter_context(nc.sbuf_tensor("hsum_row", [1, DIM], F32))

        def vsw(vector, inst):
            VC.tag(inst)
            vector.wait_ge(VC.sem, VC.n)

        def barrier():
            tn, vn, an, gn, ldn, sgn = TC.n, VC.n, AC.n, GC.n, LD.n, SG.n

            def wait_all(eng):
                eng.wait_ge(TC.sem, tn)
                eng.wait_ge(VC.sem, vn)
                eng.wait_ge(AC.sem, an)
                eng.wait_ge(GC.sem, gn)
                eng.wait_ge(LD.sem, ldn)
                eng.wait_ge(SG.sem, sgn)

            for emit in (block.tensor, block.vector, block.scalar,
                         block.gpsimd, block.sync):
                @emit
                def _(eng):
                    wait_all(eng)

        def ln_stats(src, st, ascr, rows=128, wait_ld=None, wait_v=None):
            @block.scalar
            def _(scalar):
                if wait_ld is not None:
                    scalar.wait_ge(LD.sem, wait_ld)
                if wait_v is not None:
                    scalar.wait_ge(VC.sem, wait_v)
                scalar.activation(ascr[0:rows, :], src[0:rows, :], AF.Identity,
                                  accum_out=st[0:rows, 0:1])
                AC.tagw(scalar, scalar.activation(ascr[0:rows, :], src[0:rows, :], AF.Square,
                                         accum_out=st[0:rows, 1:2]))

            a1 = AC.n

            @block.vector
            def _(vector):
                vector.wait_ge(AC.sem, a1)
                vector.tensor_single_scalar(st[0:rows, 2:3], st[0:rows, 0:1],
                                            1.0 / DIM, ALU.mult)
                vector.tensor_single_scalar(st[0:rows, 3:4], st[0:rows, 1:2],
                                            1.0 / DIM, ALU.mult)
                vsw(vector, vector.tensor_mul(st[0:rows, 4:5], st[0:rows, 2:3],
                                              st[0:rows, 2:3]))
                VC.tagw(vector, vector.tensor_sub(st[0:rows, 5:6], st[0:rows, 3:4],
                                                  st[0:rows, 4:5]))

            v1 = VC.n

            @block.scalar
            def _(scalar):
                scalar.wait_ge(VC.sem, v1)
                AC.tagw(scalar, scalar.activation(st[0:rows, 6:7], st[0:rows, 5:6],
                                                  AF.Sqrt, bias=EPS))

            a2 = AC.n

            @block.vector
            def _(vector):
                vector.wait_ge(AC.sem, a2)
                vsw(vector, vector.reciprocal(st[0:rows, 7:8], st[0:rows, 6:7]))
                VC.tagw(vector, vector.engine_nop())

            return VC.n

        @block.sync
        def _(sync):
            LD.tag(sync.dma_start(out=c_row[:], in_=cvec))
            LD.tag(sync.dma_start(out=cw_sb[:], in_=convw))
            LD.tag(sync.dma_start(out=cb_sb[:], in_=convb))
            LD.tag(sync.dma_start(out=flags[0:1, 0:1], in_=npre))
            LD.tag(sync.dma_start(out=flags[0:1, 1:2], in_=haslt))
            LD.tag(sync.dma_start(out=flags[0:1, 2:3], in_=hasrt))
            LD.tag(sync.dma_start(out=xr_row[:], in_=xr1))

        ld_p0 = LD.n

        @block.gpsimd
        def _(gpsimd):
            def lvl():
                gpsimd.wait_ge(GC.sem, GC.n)

            t = GC.tag
            t(gpsimd.memset(ident[:], 0.0))
            t(gpsimd.memset(identb[:], 0.0))
            t(gpsimd.memset(ident1[:], 1.0))
            t(gpsimd.memset(cst_a[:], -SOFTB))
            t(gpsimd.memset(cst_b[:], SCALE))
            t(gpsimd.memset(cst_c[:], CSCALE))
            t(gpsimd.memset(cst_d[:], EPS))
            t(gpsimd.memset(ones_c[:], 1.0))
            t(gpsimd.memset(ones_r[:], 1.0))
            t(gpsimd.memset(onesE[:], 0.0))
            t(gpsimd.memset(onesO[:], 0.0))
            t(gpsimd.memset(ones11[:], 1.0))
            for c in range(10):
                t(gpsimd.memset(onesVs[c][:], 0.0))
            t(gpsimd.memset(halfsel[:], 0.0))
            t(gpsimd.memset(parmask[:], 0.0))
            t(gpsimd.memset(tri[:], 1.0))
            t(gpsimd.memset(carryk[:], 0.0))
            t(gpsimd.memset(basek[:], 0.0))
            t(gpsimd.memset(qr_sb[:], 0.0))
            t(gpsimd.memset(vr_sb[:], 0.0))
            lvl()
            t(gpsimd.affine_select(out=ident[:], in_=ident[:],
                                   compare_op=ALU.not_equal,
                                   fill=1.0, base=0, pattern=[[-1, 128]],
                                   channel_multiplier=1))
            t(gpsimd.affine_select(out=identb[:], in_=identb[:],
                                   compare_op=ALU.not_equal,
                                   fill=1.0, base=0, pattern=[[-1, 128]],
                                   channel_multiplier=1))
            t(gpsimd.memset(onesE[0:64, 0:1], 1.0))
            t(gpsimd.memset(onesO[0:64, 32:33], 1.0))
            for c in range(10):
                t(gpsimd.memset(onesVs[c][0:64, c:c + 1], 1.0))
            t(gpsimd.memset(halfsel[0:1, 0:64], 1.0))
            t(gpsimd.memset(halfsel[32:33, 64:128], 1.0))
            t(gpsimd.memset(parmask[0:64, 0:1], 1.0))
            t(gpsimd.memset(parmask[64:128, 1:2], 1.0))
            t(gpsimd.affine_select(out=tri[:], in_=tri[:],
                                   compare_op=ALU.is_ge, fill=0.0, base=-2,
                                   pattern=[[1, 11]], channel_multiplier=-1))
            lvl()
            GC.tagw(gpsimd, gpsimd.memset(tri[:, 0:1], 1.0))

        g_consts = GC.n

        @block.tensor
        def _(tensor):
            tensor.wait_ge(GC.sem, g_consts)
            tensor.wait_ge(LD.sem, ld_p0)
            tensor.matmul(ps[7][:, 0:1], ones_r[:], flags[0:1, 1:2],
                          start=True, stop=True)
            TC.tagw(tensor, tensor.matmul(ps[7][:, 1:2], ones_r[:], flags[0:1, 2:3],
                                 start=True, stop=True))

        t_flags = TC.n

        @block.vector
        def _(vector):
            vector.wait_ge(TC.sem, t_flags)
            vector.tensor_copy(haslc[:], ps[7][:, 0:1])
            VC.tagw(vector, vector.tensor_copy(hasrc[:], ps[7][:, 1:2]))

        @block.scalar
        def _(scalar):
            scalar.wait_ge(LD.sem, ld_p0)
            AC.tagw(scalar, scalar.activation(c_row[:], c_row[:], AF.Silu))

        a_siluc = AC.n

        @block.tensor
        def _(tensor):
            tensor.wait_ge(AC.sem, a_siluc)
            for f in range(8):
                mm = tensor.transpose(ps[0][:, f:f + 1],
                                      c_row[0:1, 128 * f:128 * (f + 1)],
                                      ident1[:])
            TC.tagw(tensor, mm)

        t_sct = TC.n

        @block.vector
        def _(vector):
            vector.wait_ge(TC.sem, t_sct)
            VC.tagw(vector, vector.tensor_copy(sc_cols[:], ps[0][:, 0:8]))

        v_sccols = VC.n

        with ExitStack() as p0s:
            wm_buf = [p0s.enter_context(nc.sbuf_tensor(f"wm{i}", [128, 6 * DIM], F32))
                      for i in range(2)]
            wmod_ld = []
            v_modgrp = []

            for f in range(8):
                @block.sync
                def _(sync, f=f):
                    if f >= 2:
                        sync.wait_ge(VC.sem, v_modgrp[3 * (f - 2) + 2])
                    wmod_ld.append(LD.tag(sync.dma_start(
                        out=wm_buf[f % 2][:], in_=wmod[128 * f:128 * (f + 1), :])))

                t_grp = []
                for grp in range(3):
                    @block.tensor
                    def _(tensor, f=f, grp=grp):
                        if grp == 0:
                            tensor.wait_ge(LD.sem, wmod_ld[f])
                            if f == 0:
                                tensor.wait_ge(VC.sem, v_sccols)
                        if grp > 0:
                            tensor.wait_ge(VC.sem, v_modgrp[3 * f + grp - 1])
                        elif f >= 1:
                            tensor.wait_ge(VC.sem, v_modgrp[3 * (f - 1) + 2])
                        for s in range(4 * grp, 4 * grp + 4):
                            mm = tensor.matmul(
                                ps[4 + s % 4][0:1, 0:512],
                                sc_cols[:, f:f + 1],
                                wm_buf[f % 2][:, 512 * s:512 * (s + 1)],
                                start=True, stop=True)
                        t_grp.append(TC.tagw(tensor, mm))

                    @block.vector
                    def _(vector, f=f, grp=grp):
                        vector.wait_ge(TC.sem, t_grp[grp])
                        for s in range(4 * grp, 4 * grp + 4):
                            dst = mod_row[0:1, 512 * s:512 * (s + 1)]
                            if f == 0:
                                mm = vector.tensor_copy(dst,
                                                        ps[4 + s % 4][0:1, 0:512])
                            else:
                                mm = vector.tensor_add(dst, dst,
                                                       ps[4 + s % 4][0:1, 0:512])
                        v_modgrp.append(VC.tagw(vector, mm))

        barrier()

        @block.scalar
        def _(scalar):
            scalar.activation(ga_row[:], mod_row[0:1, 2 * DIM:3 * DIM], AF.Sign)
            AC.tagw(scalar, scalar.activation(c_row[:], mod_row[0:1, 2 * DIM:3 * DIM], AF.Abs))

        a_ga = AC.n

        @block.vector
        def _(vector):
            vector.wait_ge(AC.sem, a_ga)
            vsw(vector, vector.tensor_scalar_max(c_row[:], c_row[:], 0.25))
            VC.tagw(vector, vector.tensor_mul(ga_row[:], ga_row[:], c_row[:]))

        v_ga = VC.n

        @block.scalar
        def _(scalar):
            scalar.wait_ge(VC.sem, v_ga)
            scalar.activation(tmp_row[:], mod_row[0:1, 5 * DIM:6 * DIM], AF.Sign)
            AC.tagw(scalar, scalar.activation(c_row[:], mod_row[0:1, 5 * DIM:6 * DIM], AF.Abs))

        a_gb = AC.n

        @block.vector
        def _(vector):
            vector.wait_ge(AC.sem, a_gb)
            vector.tensor_scalar_max(c_row[:], c_row[:], 0.25)
            vector.tensor_scalar_add(mod_row[0:1, DIM:2 * DIM],
                                     mod_row[0:1, DIM:2 * DIM], 1.0)
            vector.tensor_mul(mod_row[0:1, 5 * DIM:6 * DIM], tmp_row[:], c_row[:])
            VC.tagw(vector, vector.tensor_scalar_add(mod_row[0:1, 4 * DIM:5 * DIM],
                                            mod_row[0:1, 4 * DIM:5 * DIM], 1.0))

        v_modrows = VC.n

        @block.tensor
        def _(tensor):
            tensor.wait_ge(VC.sem, v_modrows)
            for f in range(8):
                tensor.transpose(ps[0][:, f:f + 1],
                                 mod_row[0:1, 128 * f:128 * (f + 1)], ident1[:])
                tensor.transpose(ps[1][:, f:f + 1],
                                 mod_row[0:1, DIM + 128 * f:DIM + 128 * (f + 1)],
                                 ident1[:])
            tensor.matmul(ps[2][:, 0:512], ones_r[:],
                          mod_row[0:1, 4 * DIM:4 * DIM + 512], start=True, stop=True)
            tensor.matmul(ps[3][:, 0:512], ones_r[:],
                          mod_row[0:1, 4 * DIM + 512:5 * DIM], start=True, stop=True)
            tensor.matmul(ps[4][:, 0:512], ones_r[:],
                          mod_row[0:1, 3 * DIM:3 * DIM + 512], start=True, stop=True)
            TC.tagw(tensor, tensor.matmul(
                ps[5][:, 0:512], ones_r[:],
                mod_row[0:1, 3 * DIM + 512:4 * DIM], start=True, stop=True))

        t_reps1 = TC.n

        @block.vector
        def _(vector):
            vector.wait_ge(TC.sem, t_reps1)
            vector.tensor_copy(sa_cols[:], ps[0][:, 0:8])
            vector.tensor_copy(ca1_cols[:], ps[1][:, 0:8])
            vector.tensor_copy(cb1_rep[:, 0:512], ps[2][:, 0:512])
            vector.tensor_copy(cb1_rep[:, 512:1024], ps[3][:, 0:512])
            vector.tensor_copy(sb_rep[:, 0:512], ps[4][:, 0:512])
            VC.tagw(vector, vector.tensor_copy(sb_rep[:, 512:1024], ps[5][:, 0:512]))

        v_reps1 = VC.n

        @block.tensor
        def _(tensor):
            tensor.wait_ge(VC.sem, v_reps1)
            tensor.matmul(ps[2][:, 0:512], ones_r[:],
                          mod_row[0:1, 5 * DIM:5 * DIM + 512], start=True, stop=True)
            TC.tagw(tensor, tensor.matmul(
                ps[3][:, 0:512], ones_r[:],
                mod_row[0:1, 5 * DIM + 512:6 * DIM], start=True, stop=True))

        t_reps2 = TC.n

        @block.vector
        def _(vector):
            vector.wait_ge(TC.sem, t_reps2)
            vector.tensor_copy(gb_rep[:, 0:512], ps[2][:, 0:512])
            VC.tagw(vector, vector.tensor_copy(gb_rep[:, 512:1024], ps[3][:, 0:512]))

        barrier()

        with ExitStack() as p1s:
            e1 = p1s.enter_context
            xp = [e1(nc.sbuf_tensor(f"xp{i}", [128, DIM], F32)) for i in range(4)]
            ascr = e1(nc.sbuf_tensor("ascr", [128, DIM], BF16))
            xq = [e1(nc.sbuf_tensor(f"xq{i}", [128, DIM], F32)) for i in range(2)]
            st1p = [e1(nc.sbuf_tensor(f"st1_{i}", [128, 8], F32))
                    for i in range(4)]
            st1 = st1p[0]

            v_rn1 = ln_stats(xr_row, st1, ascr, rows=1)

            @block.vector
            def _(vector):
                VC.tagw(vector, vector.tensor_scalar(tmp_row[:], xr_row[:], st1[0:1, 2:3],
                                            st1[0:1, 7:8], ALU.subtract, ALU.mult))

            v_rn2 = VC.n

            @block.tensor
            def _(tensor):
                tensor.wait_ge(VC.sem, v_rn2)
                for f in range(8):
                    mm = tensor.transpose(ps[0][:, f:f + 1],
                                          tmp_row[0:1, 128 * f:128 * (f + 1)],
                                          ident1[:])
                TC.tagw(tensor, mm)

            t_rT = TC.n

            @block.vector
            def _(vector):
                vector.wait_ge(TC.sem, t_rT)
                vsw(vector, vector.tensor_mul(hr_cols[:], ps[0][:, 0:8],
                                              ca1_cols[:]))
                VC.tagw(vector, vector.tensor_add(hr_cols[:], hr_cols[:], sa_cols[:]))

            NPT = PRE // 128
            v_xr = [None] * NPT
            t_mm = [None] * NPT

            for i in range(NPT):
                @block.sync
                def _(sync, i=i):
                    if i >= 4:
                        sync.wait_ge(VC.sem, v_xr[i - 4])
                    LD.tag(sync.dma_start(out=xp[i % 4][:],
                                          in_=xpre[128 * i:128 * (i + 1), :]))

                ld_i = LD.n
                ln_stats(xp[i % 4], st1p[i % 4], ascr, wait_ld=ld_i,
                         wait_v=(v_rn1 if i == 0 else None))

                @block.vector
                def _(vector, i=i):
                    if i >= 2:
                        vector.wait_ge(TC.sem, t_mm[i - 2])
                    v_xr[i] = VC.tagw(vector, vector.tensor_scalar(
                        xq[i % 2][:], xp[i % 4][:], st1p[i % 4][:, 2:3],
                        st1p[i % 4][:, 7:8], ALU.subtract, ALU.mult))

                @block.tensor
                def _(tensor, i=i):
                    tensor.wait_ge(VC.sem, v_xr[i])
                    tensor.matmul(ps[1][0:1, 0:512], ones_c[:], xq[i % 2][:, 0:512],
                                  start=(i == 0), stop=(i == NPT - 1))
                    t_mm[i] = TC.tagw(tensor, tensor.matmul(
                        ps[2][0:1, 0:512], ones_c[:], xq[i % 2][:, 512:1024],
                        start=(i == 0), stop=(i == NPT - 1)))

            @block.vector
            def _(vector):
                vector.wait_ge(TC.sem, t_mm[NPT - 1])
                vector.tensor_copy(hsum_row[0:1, 0:512], ps[1][0:1, 0:512])
                vector.tensor_copy(hsum_row[0:1, 512:1024], ps[2][0:1, 0:512])
                vector.tensor_single_scalar(tmp_row[:], mod_row[0:1, 0:DIM],
                                            flags[0:1, 0:1], ALU.mult)
                vsw(vector, vector.tensor_mul(hsum_row[:], hsum_row[:],
                                              mod_row[0:1, DIM:2 * DIM]))
                VC.tagw(vector, vector.tensor_add(hsum_row[:], hsum_row[:], tmp_row[:]))

            v_hmod = VC.n

            @block.tensor
            def _(tensor):
                tensor.wait_ge(VC.sem, v_hmod)
                for f in range(8):
                    mm = tensor.transpose(ps[3][:, f:f + 1],
                                          hsum_row[0:1, 128 * f:128 * (f + 1)],
                                          ident1[:])
                TC.tagw(tensor, mm)

            t_hT = TC.n

            @block.vector
            def _(vector):
                vector.wait_ge(TC.sem, t_hT)
                VC.tagw(vector, vector.tensor_copy(hmodT[:], ps[3][:, 0:8]))

            barrier()

            wqbuf = [e1(nc.sbuf_tensor(f"wqb{i}", [128, DH], F32)) for i in range(2)]
            gl = e1(nc.sbuf_tensor("gl", [128, DH], F32))
            ll = e1(nc.sbuf_tensor("ll", [128, DH], F32))
            tb = [e1(nc.sbuf_tensor(f"tb{i}", [128, DH], BF16)) for i in range(2)]

            def emit_wq(wap, tdst, rows, cols, widx, prev_guard=None):
                ntile = rows // 128
                v_red = [None] * ntile
                for i in range(ntile):
                    @block.sync
                    def _(sync, i=i):
                        if i >= 2:
                            sync.wait_ge(VC.sem, v_red[i - 2])
                        elif prev_guard is not None:
                            sync.wait_ge(VC.sem, prev_guard[i])
                        LD.tag(sync.dma_start(out=wqbuf[i % 2][:, 0:cols],
                                              in_=wap[128 * i:128 * (i + 1), :]))

                    ldw = LD.n

                    @block.vector
                    def _(vector, i=i):
                        vector.wait_ge(LD.sem, ldw)
                        vsw(vector, vector.tensor_reduce(
                            st1[:, 0:1], wqbuf[i % 2][:, 0:cols],
                            AX.X, ALU.add, apply_absolute_value=True))
                        if i == 0:
                            v_red[i] = VC.tagw(vector, vector.tensor_copy(st1[:, 1:2],
                                                                 st1[:, 0:1]))
                        else:
                            v_red[i] = VC.tagw(vector, vector.tensor_add(
                                st1[:, 1:2], st1[:, 1:2], st1[:, 0:1]))

                @block.tensor
                def _(tensor):
                    tensor.wait_ge(VC.sem, v_red[ntile - 1])
                    TC.tagw(tensor, tensor.matmul(ps[3][0:1, 0:1], st1[:, 1:2], ones_c[:],
                                         start=True, stop=True))

                t_tot = TC.n
                b = 4 * widx

                @block.vector
                def _(vector):
                    vector.wait_ge(TC.sem, t_tot)
                    vector.tensor_single_scalar(wsc[0:1, b:b + 1], ps[3][0:1, 0:1],
                                                1.0 / (rows * cols), ALU.mult)
                    vsw(vector, vector.tensor_scalar(wsc[0:1, b + 1:b + 2],
                                                     ps[3][0:1, 0:1],
                                                     1.0 / (rows * cols), 1e-5,
                                                     ALU.mult, ALU.max))
                    vsw(vector, vector.reciprocal(wsc[0:1, b + 2:b + 3],
                                                  wsc[0:1, b + 1:b + 2]))
                    VC.tagw(vector, vector.reciprocal(wsc[0:1, b + 3:b + 4],
                                             wsc[0:1, b + 2:b + 3]))

                v_msc = VC.n

                @block.tensor
                def _(tensor):
                    tensor.wait_ge(VC.sem, v_msc)
                    tensor.matmul(ps[4][:, 0:1], ones_r[:], wsc[0:1, b + 2:b + 3],
                                  start=True, stop=True)
                    TC.tagw(tensor, tensor.matmul(ps[4][:, 1:2], ones_r[:],
                                         wsc[0:1, b + 3:b + 4],
                                         start=True, stop=True))

                t_sc2 = TC.n

                @block.vector
                def _(vector):
                    vector.wait_ge(TC.sem, t_sc2)
                    vector.tensor_copy(wsc_col[:, widx:widx + 1], ps[4][:, 0:1])
                    VC.tagw(vector, vector.tensor_copy(m_col[:, widx:widx + 1], ps[4][:, 1:2]))

                v_sccol2 = VC.n
                sg_st = [None] * ntile
                v_tq = [None] * ntile
                for i in range(ntile):
                    @block.sync
                    def _(sync, i=i):
                        if i >= 2:
                            sync.wait_ge(VC.sem, v_tq[i - 2])
                        else:
                            sync.wait_ge(VC.sem, v_red[ntile - 2 + i])
                        LD.tag(sync.dma_start(out=wqbuf[i % 2][:, 0:cols],
                                              in_=wap[128 * i:128 * (i + 1), :]))

                    ldw = LD.n

                    @block.vector
                    def _(vector, i=i):
                        vector.wait_ge(LD.sem, ldw)
                        if i == 0:
                            vector.wait_ge(VC.sem, v_sccol2)
                        if i >= 2:
                            vector.wait_ge(SG.sem, sg_st[i - 2])
                        vector.tensor_scalar(gl[:, 0:cols], wqbuf[i % 2][:, 0:cols],
                                             wsc_col[:, widx:widx + 1], 0.5,
                                             ALU.mult, ALU.is_ge)
                        vsw(vector, vector.tensor_scalar(
                            ll[:, 0:cols], wqbuf[i % 2][:, 0:cols],
                            wsc_col[:, widx:widx + 1], -0.5,
                            ALU.mult, ALU.is_le))
                        v_tq[i] = VC.tagw(vector, vector.tensor_sub(
                            tb[i % 2][:, 0:cols], gl[:, 0:cols], ll[:, 0:cols]))

                    @block.gpsimd
                    def _(gpsimd, i=i):
                        gpsimd.wait_ge(VC.sem, v_tq[i])
                        sg_st[i] = SG.tag(gpsimd.dma_start(
                            out=tdst[128 * i:128 * (i + 1), :],
                            in_=tb[i % 2][:, 0:cols]))

                return [v_tq[ntile - 2], v_tq[ntile - 1]]

            g1 = emit_wq(w1, t1d, DH, DIM, 0)
            emit_wq(w2, t2d, DIM, DH, 1, prev_guard=g1)

            @block.vector
            def _(vector):
                VC.tagw(vector, vector.tensor_single_scalar(
                    m127[:], m_col[:], 1.0 / 127.0, ALU.mult))

            barrier()

        p01.close()

        with ExitStack() as p2s:
            e2 = p2s.enter_context
            mask65 = e2(nc.sbuf_tensor("mask65", [65, 512], F32))
            ga_rep = e2(nc.sbuf_tensor("ga_rep", [128, DIM], F32))
            vsb = e2(nc.sbuf_tensor("vsb", [10, DIM], F32))
            cums_sb = e2(nc.sbuf_tensor("cums_sb", [11, DIM], F32))
            carryv = e2(nc.sbuf_tensor("carryv", [1, DIM], F32))
            woutr = [e2(nc.sbuf_tensor(f"woutr{i}", [128, DIM], F32))
                     for i in range(8)]
            xblk = [e2(nc.sbuf_tensor(f"xblk{i}", [128, DIM], F32)) for i in range(2)]
            n1t = e2(nc.sbuf_tensor("n1t", [128, DIM], F32))
            ascr2 = e2(nc.sbuf_tensor("ascr2", [128, DIM], BF16))
            st2p = [e2(nc.sbuf_tensor(f"st2_{i}", [128, 8], F32))
                    for i in range(2)]
            hT = [e2(nc.sbuf_tensor(f"hT{i}", [128, 512], F32)) for i in range(8)]
            wqb = [e2(nc.sbuf_tensor(f"wqkvb{i}", [128, 512], F32)) for i in range(2)]
            qsl = [e2(nc.sbuf_tensor(f"qsl{i}", [128, 512], F32)) for i in range(8)]
            ksl = [e2(nc.sbuf_tensor(f"ksl{i}", [128, 512], F32)) for i in range(8)]
            vext = [e2(nc.sbuf_tensor(f"vext{i}", [65, DIM], F32)) for i in range(8)]
            Pe2 = [e2(nc.sbuf_tensor(f"Pe{i}", [65, 512], F32))
                   for i in range(2)]
            Po2 = [e2(nc.sbuf_tensor(f"Po{i}", [65, 512], F32))
                   for i in range(2)]
            rcb = [e2(nc.sbuf_tensor(f"rcb{i}", [33, 512], F32))
                   for i in range(2)]
            ao = [e2(nc.sbuf_tensor(f"ao{i}", [128, 512], F32)) for i in range(8)]
            ksc = e2(nc.sbuf_tensor("ksc", [128, 22], F32))
            kx = e2(nc.sbuf_tensor("kx", [128, 11], F32))
            kpadE = e2(nc.sbuf_tensor("kpadE", [128, 64], F32))
            kpadO = e2(nc.sbuf_tensor("kpadO", [128, 64], F32))
            sgm = e2(nc.sbuf_tensor("sgm", [33, 8], F32))

            @block.gpsimd
            def _(gpsimd):
                GC.tag(gpsimd.memset(sgm[:], 0.0))
                GC.tag(gpsimd.memset(mask65[:], 1.0))
                gpsimd.wait_ge(GC.sem, GC.n)
                GC.tagw(gpsimd, gpsimd.affine_select(
                    out=mask65[0:64, :], in_=mask65[0:64, :],
                    compare_op=ALU.is_ge, fill=0.0, base=0,
                    pattern=[[0, 8], [1, 64]], channel_multiplier=-1))

            g_p2c = GC.n

            @block.sync
            def _(sync):
                for i in range(8):
                    LD.tag(sync.dma_start(out=woutr[i][:],
                                          in_=woutd[128 * i:128 * (i + 1), :]))

            ld_wout = LD.n

            @block.tensor
            def _(tensor):
                tensor.matmul(ps[0][:, 0:512], ones_r[:], ga_row[0:1, 0:512],
                              start=True, stop=True)
                TC.tagw(tensor, tensor.matmul(ps[1][:, 0:512], ones_r[:],
                                     ga_row[0:1, 512:1024],
                                     start=True, stop=True))

            t_gar = TC.n

            @block.vector
            def _(vector):
                vector.wait_ge(TC.sem, t_gar)
                vector.tensor_copy(ga_rep[:, 0:512], ps[0][:, 0:512])
                VC.tagw(vector, vector.tensor_copy(ga_rep[:, 512:1024], ps[1][:, 0:512]))

            NBLK = len(ABLK)
            for bi, (off, Tb) in enumerate(ABLK):
                nt = Tb // 128
                nb = Tb // 64
                last_blk = (bi == NBLK - 1)

                t_tr = [None] * nt
                v_hTe = [None] * nt
                v_n1l = [None] * nt
                for i in range(nt):
                    @block.sync
                    def _(sync, i=i):
                        if i >= 2:
                            sync.wait_ge(VC.sem, v_n1l[i - 2])
                        LD.tag(sync.dma_start(
                            out=xblk[i % 2][:],
                            in_=xext[off + 128 * i:off + 128 * (i + 1), :]))

                    ld_i = LD.n
                    ln_stats(xblk[i % 2], st2p[i % 2], ascr2, wait_ld=ld_i)

                    @block.vector
                    def _(vector, i=i):
                        if i >= 2:
                            vector.wait_ge(TC.sem, t_tr[i - 2])
                        v_n1l[i] = VC.tagw(vector, vector.tensor_scalar(
                            n1t[:], xblk[i % 2][:], st2p[i % 2][:, 2:3],
                            st2p[i % 2][:, 7:8], ALU.subtract, ALU.mult))

                    @block.tensor
                    def _(tensor, i=i):
                        tensor.wait_ge(VC.sem, v_n1l[i])
                        if i >= 2:
                            tensor.wait_ge(VC.sem, v_hTe[i - 2])
                        for f in range(8):
                            bank = 2 + (i % 2) * 2 + f // 4
                            mm = tensor.transpose(
                                ps[bank][:, 128 * (f % 4):128 * (f % 4) + 128],
                                n1t[:, 128 * f:128 * (f + 1)], ident[:])
                        t_tr[i] = TC.tagw(tensor, mm)

                    @block.vector
                    def _(vector, i=i):
                        vector.wait_ge(TC.sem, t_tr[i])
                        for f in range(8):
                            bank = 2 + (i % 2) * 2 + f // 4
                            mm = vector.tensor_scalar(
                                hT[f][:, 128 * i:128 * (i + 1)],
                                ps[bank][:, 128 * (f % 4):128 * (f % 4) + 128],
                                ca1_cols[:, f:f + 1], sa_cols[:, f:f + 1],
                                ALU.mult, ALU.add)
                        v_hTe[i] = VC.tagw(vector, mm)

                if bi == 0:
                    @block.vector
                    def _(vector):
                        vector.wait_ge(VC.sem, v_hTe[nt - 1])
                        for f in range(8):
                            mm = vector.tensor_scalar_mul(
                                hT[f][:, 0:128], hT[f][:, 0:128], haslc[:, 0:1])
                        VC.tagw(vector, mm)
                    v_hT_done = VC.n
                else:
                    v_hT_done = v_hTe[nt - 1]

                last_evac = v_hT_done
                v_radd = []
                prev_t_qf = None
                for og in range(6):
                    ld_f = []
                    t_qf = [None] * 8
                    extra = (bi == 0 and og in (2, 3)) or                             (last_blk and og in (0, 1, 4, 5))
                    rtok = (last_blk and og in (0, 1, 4, 5))
                    for f in range(8):
                        @block.sync
                        def _(sync, f=f, og=og, prev_t_qf=prev_t_qf):
                            if f >= 2:
                                sync.wait_ge(TC.sem, t_qf[f - 2])
                            elif prev_t_qf is not None:
                                sync.wait_ge(TC.sem, prev_t_qf[6 + f])
                            ld_f.append(LD.tag(sync.dma_start(
                                out=wqb[f % 2][:],
                                in_=wqkv[128 * f:128 * (f + 1),
                                         512 * og:512 * (og + 1)])))

                        @block.tensor
                        def _(tensor, f=f, og=og, extra=extra, rtok=rtok):
                            tensor.wait_ge(LD.sem, ld_f[f])
                            if f == 0:
                                tensor.wait_ge(VC.sem, last_evac)
                                if bi == 0 and og == 0:
                                    tensor.wait_ge(GC.sem, g_p2c)
                            if extra and len(v_radd) >= 2:
                                tensor.wait_ge(VC.sem, v_radd[-2])
                            if og < 4:
                                for o4 in range(4):
                                    mm = tensor.matmul(
                                        ps[o4][:, 0:Tb],
                                        wqb[f % 2][:, 128 * o4:128 * (o4 + 1)],
                                        hT[f][:, 0:Tb],
                                        start=(f == 0), stop=(f == 7))
                                if bi == 0 and og in (2, 3):
                                    for o4 in range(4):
                                        mm = tensor.matmul(
                                            ps[4 + f % 2][:, o4:o4 + 1],
                                            wqb[f % 2][:, 128 * o4:128 * (o4 + 1)],
                                            hmodT[:, f:f + 1],
                                            start=(o4 == 0), stop=(o4 == 3))
                            else:
                                for tc in range(nt):
                                    mm = tensor.matmul(
                                        ps[tc][:, 0:512],
                                        hT[f][:, 128 * tc:128 * (tc + 1)],
                                        wqb[f % 2][:, 0:512],
                                        start=(f == 0), stop=(f == 7))
                                if bi == 0 and og in (4, 5):
                                    mm = tensor.matmul(
                                        ps[6][0:1, 0:512], hmodT[:, f:f + 1],
                                        wqb[f % 2][:, 0:512],
                                        start=(f == 0), stop=(f == 7))
                            if rtok:
                                for o4 in range(4):
                                    mm = tensor.matmul(
                                        ps[4 + f % 2][:, 4 + o4:5 + o4],
                                        wqb[f % 2][:, 128 * o4:128 * (o4 + 1)],
                                        hr_cols[:, f:f + 1],
                                        start=(o4 == 0), stop=(o4 == 3))
                            t_qf[f] = TC.tagw(tensor, mm)

                        if extra:
                            @block.vector
                            def _(vector, f=f, og=og, rtok=rtok):
                                vector.wait_ge(TC.sem, t_qf[f])
                                if bi == 0 and og in (2, 3):
                                    bofs = (og - 2) * 4
                                    mm = vector.tensor_add(
                                        basek[:, bofs:bofs + 4],
                                        basek[:, bofs:bofs + 4],
                                        ps[4 + f % 2][:, 0:4])
                                if rtok:
                                    dst = qr_sb if og < 2 else vr_sb
                                    bofs = (og % 2) * 4
                                    mm = vector.tensor_add(
                                        dst[:, bofs:bofs + 4],
                                        dst[:, bofs:bofs + 4],
                                        ps[4 + f % 2][:, 4:8])
                                v_radd.append(VC.tagw(vector, mm))

                    @block.vector
                    def _(vector, og=og):
                        vector.wait_ge(TC.sem, t_qf[7])
                        if og < 4:
                            for o4 in range(4):
                                dst = (qsl if og < 2 else ksl)[(og % 2) * 4 + o4]
                                mm = vector.tensor_copy(dst[:, 0:Tb],
                                                        ps[o4][:, 0:Tb])
                        else:
                            vc0 = 512 * (og - 4)
                            for tc in range(nt):
                                vector.tensor_copy(
                                    vext[2 * tc][0:64, vc0:vc0 + 512],
                                    ps[tc][0:64, 0:512])
                                mm = vector.tensor_copy(
                                    vext[2 * tc + 1][0:64, vc0:vc0 + 512],
                                    ps[tc][64:128, 0:512])
                            if bi == 0 and og in (4, 5):
                                mm = vector.tensor_copy(carryv[0:1, vc0:vc0 + 512],
                                                        ps[6][0:1, 0:512])
                        VC.tagw(vector, mm)
                    last_evac = VC.n
                    prev_t_qf = t_qf

                if bi == 0:
                    @block.vector
                    def _(vector):
                        vector.wait_ge(VC.sem, last_evac)
                        VC.tagw(vector, vector.tensor_copy(carryk[:], basek[:]))
                    last_evac = VC.n

                @block.tensor
                def _(tensor, nb=nb):
                    tensor.wait_ge(VC.sem, last_evac)
                    for c in range(nb):
                        for s in range(2):
                            mm = tensor.matmul(ps[s][0:10, 0:512],
                                               onesVs[c][0:64, 0:10],
                                               vext[c][0:64, 512 * s:512 * (s + 1)],
                                               start=(c == 0), stop=(c == nb - 1))
                    TC.tagw(tensor, mm)

                t_vs = TC.n

                @block.vector
                def _(vector):
                    vector.wait_ge(TC.sem, t_vs)
                    vector.tensor_copy(vsb[0:10, 0:512], ps[0][0:10, 0:512])
                    VC.tagw(vector, vector.tensor_copy(vsb[0:10, 512:1024],
                                              ps[1][0:10, 0:512]))

                v_vsb = VC.n

                @block.tensor
                def _(tensor):
                    tensor.wait_ge(VC.sem, v_vsb)
                    tensor.matmul(ps[2][0:11, 0:512], tri[:], vsb[0:10, 0:512],
                                  start=True, stop=False)
                    tensor.matmul(ps[2][0:11, 0:512], ones11[:],
                                  carryv[0:1, 0:512], start=False, stop=True)
                    tensor.matmul(ps[3][0:11, 0:512], tri[:], vsb[0:10, 512:1024],
                                  start=True, stop=False)
                    TC.tagw(tensor, tensor.matmul(ps[3][0:11, 0:512], ones11[:],
                                         carryv[0:1, 512:1024],
                                         start=False, stop=True))

                t_cumv = TC.n

                @block.vector
                def _(vector):
                    vector.wait_ge(TC.sem, t_cumv)
                    vector.tensor_single_scalar(cums_sb[0:11, 0:512],
                                                ps[2][0:11, 0:512],
                                                VSCALE, ALU.mult)
                    vector.tensor_single_scalar(cums_sb[0:11, 512:1024],
                                                ps[3][0:11, 0:512],
                                                VSCALE, ALU.mult)
                    vector.tensor_single_scalar(carryv[0:1, 0:512],
                                                ps[2][0:1, 0:512],
                                                1.0, ALU.mult)
                    VC.tagw(vector, vector.tensor_single_scalar(carryv[0:1, 512:1024],
                                                       ps[3][0:1, 0:512],
                                                       1.0, ALU.mult))

                v_cumsb = VC.n
                sg_vf = []

                @block.gpsimd
                def _(gpsimd, nb=nb):
                    gpsimd.wait_ge(VC.sem, v_cumsb)
                    for c in range(nb):
                        sg_vf.append(SG.tag(gpsimd.dma_start(
                            out=vext[c][64:65, :], in_=cums_sb[c + 1:c + 2, :])))

                v_cumv = VC.n
                sg_vfill = SG.n

                v_ao_last = v_cumv
                prev_a_exp = None
                Wb = 64 * nb
                v_kpad_l = [None] * 8

                @block.vector
                def _(vector, nb=nb):
                    vector.wait_ge(VC.sem, v_cumv)
                    for hp in range(8):
                        kv = ksl[hp][:].rearrange("p (c t) -> p c t", t=64)
                        vector.tensor_reduce(ksc[:, 0:nb, None],
                                             kv[:, 0:nb, :], AX.X, ALU.add)
                        vector.tensor_copy(kx[:, 0:1], carryk[:, hp:hp + 1])
                        vsw(vector, vector.tensor_tensor_scan(
                            ksc[:, 10:10 + nb], ksc[:, 0:nb],
                            ksc[:, 0:nb], carryk[:, hp:hp + 1],
                            ALU.add, ALU.bypass))
                        if nb > 1:
                            vector.tensor_copy(kx[:, 1:nb],
                                               ksc[:, 10:10 + nb - 1])
                        vector.tensor_copy(carryk[:, hp:hp + 1],
                                           ksc[:, 9 + nb:10 + nb])
                        vector.tensor_tensor(
                            kpadE[:, 8 * hp:8 * hp + nb], kx[:, 0:nb],
                            parmask[:, 0:1].broadcast_to([128, nb]), ALU.mult)
                        v_kpad_l[hp] = VC.tagw(vector, vector.tensor_tensor(
                            kpadO[:, 8 * hp:8 * hp + nb], kx[:, 0:nb],
                            parmask[:, 1:2].broadcast_to([128, nb]), ALU.mult))
                t_sc_l = [None] * 8
                a_exp_l = [None] * 8
                v_mask_l = [None] * 8
                t_dn_l = [None] * 8
                v_rc_l = [None] * 8
                t_rp_l = [None] * 8
                v_norm_l = [None] * 8
                t_ao_l = [None] * 8
                v_evac_l = [None] * 8

                for step in range(9):
                    hp = step
                    pm = step - 1
                    if hp < 8:
                        sb_ = hp % 2

                        @block.tensor
                        def _(tensor, hp=hp, nb=nb, sb_=sb_,
                              v_ao_last=v_ao_last, prev_a_exp=prev_a_exp):
                            tensor.wait_ge(VC.sem, v_kpad_l[hp])
                            if prev_a_exp is None:
                                tensor.wait_ge(VC.sem, v_ao_last)
                            else:
                                tensor.wait_ge(AC.sem, prev_a_exp)
                            tensor.wait_ge(SG.sem, sg_vfill)
                            for c in range(nb):
                                cc = slice(64 * c, 64 * (c + 1))
                                tensor.matmul(ps[sb_][0:64, cc],
                                              ksl[hp][0:64, cc], qsl[hp][0:64, cc],
                                              start=(c == 0), stop=(c == nb - 1),
                                              tile_position=(0, 0))
                            for c in range(nb):
                                cc = slice(64 * c, 64 * (c + 1))
                                tensor.matmul(ps[sb_][64:128, cc],
                                              ksl[hp][64:128, cc],
                                              qsl[hp][64:128, cc],
                                              start=(c == 0), stop=(c == nb - 1),
                                              tile_position=(64, 64))
                            for c in range(nb):
                                cc = slice(64 * c, 64 * (c + 1))
                                tensor.matmul(ps[2][0:1, cc],
                                              kpadE[:, 8 * hp + c:8 * hp + c + 1],
                                              qsl[hp][:, cc],
                                              start=(c == 0), stop=(c == nb - 1),
                                              tile_position=(0, 0))
                                mm = tensor.matmul(ps[2][32:33, cc],
                                                   kpadO[:, 8 * hp + c:
                                                          8 * hp + c + 1],
                                                   qsl[hp][:, cc],
                                                   start=(c == 0),
                                                   stop=(c == nb - 1),
                                                   tile_position=(0, 32))
                            t_sc_l[hp] = TC.tagw(tensor, mm)

                        @block.scalar
                        def _(scalar, hp=hp, Wb=Wb, sb_=sb_):
                            scalar.wait_ge(TC.sem, t_sc_l[hp])
                            Pe, Po = Pe2[hp % 2], Po2[hp % 2]
                            scalar.activation(Pe[0:64, 0:Wb], ps[sb_][0:64, 0:Wb],
                                              AF.Exp, bias=-SOFTB, scale=SCALE)
                            scalar.activation(Po[0:64, 0:Wb],
                                              ps[sb_][64:128, 0:Wb],
                                              AF.Exp, bias=-SOFTB, scale=SCALE)
                            scalar.activation(Pe[64:65, 0:Wb], ps[2][0:1, 0:Wb],
                                              AF.Sigmoid, scale=CSCALE)
                            a_exp_l[hp] = AC.tagw(scalar, scalar.activation(
                                Po[64:65, 0:Wb], ps[2][32:33, 0:Wb],
                                AF.Sigmoid, scale=CSCALE))

                        prev_a_exp = a_exp_l[hp]

                        @block.vector
                        def _(vector, hp=hp, Wb=Wb):
                            vector.wait_ge(AC.sem, a_exp_l[hp])
                            Pe, Po = Pe2[hp % 2], Po2[hp % 2]
                            vector.tensor_mul(Pe[0:64, 0:Wb], Pe[0:64, 0:Wb],
                                              mask65[0:64, 0:Wb])
                            v_mask_l[hp] = VC.tagw(vector, vector.tensor_mul(
                                Po[0:64, 0:Wb], Po[0:64, 0:Wb],
                                mask65[0:64, 0:Wb]))

                        @block.tensor
                        def _(tensor, hp=hp, Wb=Wb):
                            tensor.wait_ge(VC.sem, v_mask_l[hp])
                            Pe, Po = Pe2[hp % 2], Po2[hp % 2]
                            tensor.matmul(ps[3][0:33, 0:Wb], onesE[:],
                                          Pe[0:64, 0:Wb], start=True, stop=False)
                            t_dn_l[hp] = TC.tagw(tensor, tensor.matmul(
                                ps[3][0:33, 0:Wb], onesO[:], Po[0:64, 0:Wb],
                                start=False, stop=True))

                    if pm >= 0:
                        @block.tensor
                        def _(tensor, pm=pm, Wb=Wb):
                            tensor.wait_ge(VC.sem, v_rc_l[pm])
                            rc = rcb[pm % 2]
                            tensor.matmul(ps[4][0:64, 0:Wb], ones_r[0:1, 0:64],
                                          rc[0:1, 0:Wb], start=True, stop=True,
                                          tile_position=(0, 0))
                            t_rp_l[pm] = TC.tagw(tensor, tensor.matmul(
                                ps[4][64:128, 0:Wb], halfsel[32:33, 64:128],
                                rc[32:33, 0:Wb], start=True, stop=True,
                                tile_position=(32, 64)))

                        @block.vector
                        def _(vector, pm=pm, Wb=Wb):
                            vector.wait_ge(TC.sem, t_rp_l[pm])
                            Pe, Po = Pe2[pm % 2], Po2[pm % 2]
                            vector.tensor_mul(Pe[0:64, 0:Wb], Pe[0:64, 0:Wb],
                                              ps[4][0:64, 0:Wb])
                            v_norm_l[pm] = VC.tagw(vector, vector.tensor_mul(
                                Po[0:64, 0:Wb], Po[0:64, 0:Wb],
                                ps[4][64:128, 0:Wb]))

                        @block.tensor
                        def _(tensor, pm=pm, nb=nb):
                            tensor.wait_ge(VC.sem, v_norm_l[pm])
                            ab_ = 5 + pm % 2
                            Pe, Po = Pe2[pm % 2], Po2[pm % 2]
                            for c in range(nb):
                                cc = slice(64 * c, 64 * (c + 1))
                                tensor.matmul(ps[ab_][0:64, cc],
                                              vext[c][0:65,
                                                      128 * pm:128 * pm + 64],
                                              Pe[0:65, cc],
                                              start=(c == 0), stop=(c == nb - 1),
                                              tile_position=(0, 0))
                            for c in range(nb):
                                cc = slice(64 * c, 64 * (c + 1))
                                mm = tensor.matmul(
                                    ps[ab_][64:128, cc],
                                    vext[c][0:65, 128 * pm + 64:128 * (pm + 1)],
                                    Po[0:65, cc],
                                    start=(c == 0), stop=(c == nb - 1),
                                    tile_position=(0, 64))
                            t_ao_l[pm] = TC.tagw(tensor, mm)

                    if hp < 8:
                        @block.vector
                        def _(vector, hp=hp, Wb=Wb):
                            vector.wait_ge(TC.sem, t_dn_l[hp])
                            rc = rcb[hp % 2]
                            vector.reciprocal(rc[0:1, 0:Wb], ps[3][0:1, 0:Wb])
                            v_rc_l[hp] = VC.tagw(vector, vector.reciprocal(
                                rc[32:33, 0:Wb], ps[3][32:33, 0:Wb]))

                    if pm >= 0:
                        @block.vector
                        def _(vector, pm=pm, Wb=Wb):
                            vector.wait_ge(TC.sem, t_ao_l[pm])
                            ab_ = 5 + pm % 2
                            v_evac_l[pm] = VC.tagw(vector, vector.tensor_copy(
                                ao[pm][:, 0:Wb], ps[ab_][:, 0:Wb]))

                v_ao_last = v_evac_l[7]

                v_x1 = [None] * nt
                sg_x1 = [None] * nt
                for tc in range(nt):
                    @block.sync
                    def _(sync, tc=tc):
                        if tc >= 2:
                            sync.wait_ge(SG.sem, sg_x1[tc - 2])
                        else:
                            sync.wait_ge(VC.sem, v_ao_last)
                        LD.tag(sync.dma_start(
                            out=xblk[tc % 2][:],
                            in_=xext[off + 128 * tc:off + 128 * (tc + 1), :]))

                    ld_res = LD.n

                    @block.tensor
                    def _(tensor, tc=tc, v_ao_last=v_ao_last):
                        tensor.wait_ge(VC.sem, v_ao_last)
                        if bi == 0 and tc == 0:
                            tensor.wait_ge(LD.sem, ld_wout)
                        if tc >= 2:
                            tensor.wait_ge(VC.sem, v_x1[tc - 2])
                        pb = (tc % 2) * 2
                        for hp in range(8):
                            tensor.matmul(ps[pb][:, 0:512],
                                          ao[hp][:, 128 * tc:128 * (tc + 1)],
                                          woutr[hp][:, 0:512],
                                          start=(hp == 0), stop=(hp == 7))
                            mm = tensor.matmul(ps[pb + 1][:, 0:512],
                                               ao[hp][:, 128 * tc:128 * (tc + 1)],
                                               woutr[hp][:, 512:1024],
                                               start=(hp == 0), stop=(hp == 7))
                        TC.tagw(tensor, mm)

                    t_y = TC.n

                    @block.vector
                    def _(vector, tc=tc, t_y=t_y, ld_res=ld_res):
                        vector.wait_ge(TC.sem, t_y)
                        vector.wait_ge(LD.sem, ld_res)
                        pb = (tc % 2) * 2
                        vector.tensor_mul(n1t[:, 0:512], ps[pb][:, 0:512],
                                          ga_rep[:, 0:512])
                        vector.tensor_mul(n1t[:, 512:1024], ps[pb + 1][:, 0:512],
                                          ga_rep[:, 512:1024])
                        vector.tensor_add(xblk[tc % 2][:, 0:512],
                                          xblk[tc % 2][:, 0:512], n1t[:, 0:512])
                        v_x1[tc] = VC.tagw(vector, vector.tensor_add(
                            xblk[tc % 2][:, 512:1024],
                            xblk[tc % 2][:, 512:1024], n1t[:, 512:1024]))

                    @block.gpsimd
                    def _(gpsimd, tc=tc):
                        gpsimd.wait_ge(VC.sem, v_x1[tc])
                        sg_x1[tc] = SG.tag(gpsimd.dma_start(
                            out=x1d[off + 128 * tc:off + 128 * (tc + 1), :],
                            in_=xblk[tc % 2][:]))

                barrier()

            @block.vector
            def _(vector):
                vector.tensor_tensor(kpadE[:, 0:8], carryk[:],
                                     parmask[:, 0:1].broadcast_to([128, 8]),
                                     ALU.mult)
                VC.tagw(vector, vector.tensor_tensor(
                    kpadO[:, 0:8], carryk[:],
                    parmask[:, 1:2].broadcast_to([128, 8]), ALU.mult))

            v_cpad = VC.n

            @block.tensor
            def _(tensor):
                tensor.wait_ge(VC.sem, v_cpad)
                for hp in range(8):
                    tensor.matmul(ps[0][0:1, hp:hp + 1], kpadE[:, hp:hp + 1],
                                  qr_sb[:, hp:hp + 1],
                                  start=(hp == 0), stop=(hp == 7),
                                  tile_position=(0, 0))
                    tensor.matmul(ps[0][32:33, hp:hp + 1], kpadO[:, hp:hp + 1],
                                  qr_sb[:, hp:hp + 1],
                                  start=(hp == 0), stop=(hp == 7),
                                  tile_position=(0, 32))
                for f in range(8):
                    mm = tensor.transpose(ps[1][:, f:f + 1],
                                          carryv[0:1, 128 * f:128 * (f + 1)],
                                          ident1[:])
                TC.tagw(tensor, mm)

            t_rcd = TC.n

            @block.scalar
            def _(scalar):
                scalar.wait_ge(TC.sem, t_rcd)
                scalar.activation(sgm[0:1, :], ps[0][0:1, 0:8],
                                  AF.Sigmoid, scale=CSCALE)
                AC.tagw(scalar, scalar.activation(sgm[32:33, :], ps[0][32:33, 0:8],
                                         AF.Sigmoid, scale=CSCALE))

            a_rsg = AC.n

            @block.tensor
            def _(tensor):
                tensor.wait_ge(AC.sem, a_rsg)
                TC.tagw(tensor, tensor.matmul(ps[2][:, 0:8], halfsel[:], sgm[:],
                                     start=True, stop=True))

            t_rsg = TC.n

            @block.vector
            def _(vector):
                vector.wait_ge(TC.sem, t_rsg)
                vsw(vector, vector.tensor_copy(ksc[:, 0:8], ps[1][:, 0:8]))
                vsw(vector, vector.tensor_mul(ksc[:, 0:8], ksc[:, 0:8],
                                              ps[2][:, 0:8]))
                vsw(vector, vector.tensor_scalar_mul(ksc[:, 0:8], ksc[:, 0:8],
                                                     VSCALE))
                VC.tagw(vector, vector.tensor_add(ksc[:, 0:8], ksc[:, 0:8], vr_sb[:]))

            v_rout = VC.n

            @block.tensor
            def _(tensor):
                tensor.wait_ge(VC.sem, v_rout)
                for hp in range(8):
                    tensor.matmul(ps[3][0:1, 0:512], ksc[:, hp:hp + 1],
                                  woutr[hp][:, 0:512],
                                  start=(hp == 0), stop=(hp == 7))
                    mm = tensor.matmul(ps[4][0:1, 0:512], ksc[:, hp:hp + 1],
                                       woutr[hp][:, 512:1024],
                                       start=(hp == 0), stop=(hp == 7))
                TC.tagw(tensor, mm)

            t_ry = TC.n

            @block.vector
            def _(vector):
                vector.wait_ge(TC.sem, t_ry)
                vector.tensor_mul(tmp_row[0:1, 0:512], ps[3][0:1, 0:512],
                                  ga_row[0:1, 0:512])
                vector.tensor_mul(tmp_row[0:1, 512:1024], ps[4][0:1, 0:512],
                                  ga_row[0:1, 512:1024])
                vector.tensor_add(xr_row[0:1, 0:512], xr_row[0:1, 0:512],
                                  tmp_row[0:1, 0:512])
                vector.tensor_add(xr_row[0:1, 512:1024], xr_row[0:1, 512:1024],
                                  tmp_row[0:1, 512:1024])
                VC.tagw(vector, vector.memset(n1t[:], 0.0))

            v_rx1 = VC.n

            @block.gpsimd
            def _(gpsimd):
                gpsimd.wait_ge(VC.sem, v_rx1)
                SG.tag(gpsimd.dma_start(out=x1d[2176:2177, :], in_=xr_row[:]))
                SG.tag(gpsimd.dma_start(out=x1d[2177:2304, :], in_=n1t[0:127, :]))

            barrier()

        with ExitStack() as p3s:
            e3 = p3s.enter_context
            t1T = [e3(nc.sbuf_tensor(f"t1T{f}", [128, DH], BF16)) for f in range(8)]
            t2T = [e3(nc.sbuf_tensor(f"t2T{h}", [128, DIM], BF16)) for h in range(32)]
            x1b = [e3(nc.sbuf_tensor(f"x1b{i}", [128, DIM], F32)) for i in range(2)]
            st3p = [e3(nc.sbuf_tensor(f"st3_{i}", [128, 8], F32))
                    for i in range(2)]
            st3 = st3p[0]


            ld_t = []

            @block.sync
            def _(sync):
                for f in range(8):
                    ld_t.append(LD.tag(sync.dma_start_transpose(
                        t1T[f][:], t1d[:, 128 * f:128 * (f + 1)])))
                for h in range(32):
                    ld_t.append(LD.tag(sync.dma_start_transpose(
                        t2T[h][:], t2d[:, 128 * h:128 * (h + 1)])))

            ld_tw = LD.n

            with ExitStack() as pks:
                ek = pks.enter_context
                n2t = ek(nc.sbuf_tensor("n2t", [128, DIM], F32))
                h2t = ek(nc.sbuf_tensor("h2t", [128, DIM], F32))
                kqi = ek(nc.sbuf_tensor("kqi", [128, DIM], I32))
                kqf = ek(nc.sbuf_tensor("kqf", [128, DIM], F32))
                cqt = ek(nc.sbuf_tensor("cqt", [128, DIM], F32))
                kqb = [ek(nc.sbuf_tensor(f"kqb{i}", [128, DIM], F32))
                       for i in range(2)]
                kqstg = [ek(nc.sbuf_tensor(f"kqstg{i}", [128, DIM], BF16))
                         for i in range(2)]
                m1stg = ek(nc.sbuf_tensor("m1stg", [1, 512], F32))
                ascr3 = ek(nc.sbuf_tensor("ascr3", [128, DIM], BF16))
                qs3 = ek(nc.sbuf_tensor("qs3", [128, 2], F32))
                scl = ek(nc.sbuf_tensor("scl", [128, 18], F32))
                m1sc = ek(nc.sbuf_tensor("m1sc", [128, 18], F32))

                kqdv = kqd.rearrange("(f p) t -> p f t", f=8)
                NXT = X1R // 128
                v_kq = [None] * NXT
                t_kqT = [None] * NXT
                v_kqTe = [None] * NXT
                sg_kq = [None] * NXT
                for i in range(NXT):
                    @block.sync
                    def _(sync, i=i):
                        if i >= 2:
                            sync.wait_ge(VC.sem, v_kq[i - 2])
                        LD.tag(sync.dma_start(out=x1b[i % 2][:],
                                              in_=x1d[128 * i:128 * (i + 1), :]))

                    ld_i = LD.n
                    ln_stats(x1b[i % 2], st3p[i % 2], ascr3, wait_ld=ld_i)

                    @block.vector
                    def _(vector, i=i):
                        vsw(vector, vector.tensor_scalar(
                            n2t[:], x1b[i % 2][:], st3p[i % 2][:, 2:3],
                            st3p[i % 2][:, 7:8], ALU.subtract, ALU.mult))
                        vsw(vector, vector.tensor_mul(h2t[:], n2t[:],
                                                      cb1_rep[:]))
                        vsw(vector, vector.tensor_add(h2t[:], h2t[:], sb_rep[:]))
                        vsw(vector, vector.tensor_reduce(
                            qs3[:, 0:1], h2t[:], AX.X, ALU.max,
                            apply_absolute_value=True))
                        vector.tensor_scalar(qs3[:, 1:2], qs3[:, 0:1], 1e-5,
                                             1.0 / 127.0, ALU.max, ALU.mult)
                        vector.tensor_scalar(m1sc[:, i:i + 1], qs3[:, 0:1], 1e-5,
                                             m127[:, 0:1], ALU.max, ALU.mult)
                        vsw(vector, vector.reciprocal(scl[:, i:i + 1],
                                                      qs3[:, 1:2]))
                        vsw(vector, vector.tensor_scalar_mul(n2t[:], h2t[:],
                                                             scl[:, i:i + 1]))
                        vsw(vector, vector.tensor_copy(kqi[:], n2t[:]))
                        vsw(vector, vector.tensor_copy(kqf[:], kqi[:]))
                        vsw(vector, vector.tensor_sub(cqt[:], n2t[:], kqf[:]))
                        vector.tensor_single_scalar(h2t[:], cqt[:], -0.5, ALU.is_le)
                        vsw(vector, vector.scalar_tensor_tensor(
                            cqt[:], cqt[:], 0.5, kqf[:], ALU.is_ge, ALU.add))
                        if i >= 2:
                            vector.wait_ge(TC.sem, t_kqT[i - 2])
                        v_kq[i] = VC.tagw(vector, vector.tensor_sub(
                            kqb[i % 2][:], cqt[:], h2t[:]))

                    @block.tensor
                    def _(tensor, i=i):
                        tensor.wait_ge(VC.sem, v_kq[i])
                        if i >= 2:
                            tensor.wait_ge(VC.sem, v_kqTe[i - 2])
                        for f in range(8):
                            bank = (i % 2) * 2 + f // 4
                            mm = tensor.transpose(
                                ps[bank][:, 128 * (f % 4):128 * (f % 4) + 128],
                                kqb[i % 2][:, 128 * f:128 * (f + 1)], ident[:])
                        t_kqT[i] = TC.tagw(tensor, mm)

                    @block.vector
                    def _(vector, i=i):
                        vector.wait_ge(TC.sem, t_kqT[i])
                        if i >= 2:
                            vector.wait_ge(SG.sem, sg_kq[i - 2])
                        haloL = (i == 0)
                        haloR = (i == 17)
                        for f in range(8):
                            bank = (i % 2) * 2 + f // 4
                            mm = vector.tensor_copy(
                                kqstg[i % 2][:, 128 * f:128 * (f + 1)],
                                ps[bank][:, 128 * (f % 4):128 * (f % 4) + 128])
                        kv = kqstg[i % 2][:].rearrange("p (f t) -> p f t", t=128)
                        if haloL or haloR:
                            vsw(vector, mm)
                        if haloL:
                            mm = vector.tensor_scalar_mul(kv[:, :, 127:128],
                                                          kv[:, :, 127:128],
                                                          haslc[:, 0:1])
                        if haloR:
                            mm = vector.tensor_scalar_mul(kv[:, :, 0:1],
                                                          kv[:, :, 0:1],
                                                          hasrc[:, 0:1])
                        v_kqTe[i] = VC.tagw(vector, mm)

                    @block.gpsimd
                    def _(gpsimd, i=i):
                        gpsimd.wait_ge(VC.sem, v_kqTe[i])
                        sg_kq[i] = SG.tag(gpsimd.dma_start(
                            out=kqdv[:, :, 128 * i:128 * (i + 1)],
                            in_=kqstg[i % 2][:].rearrange("p (f t) -> p f t", f=8)))

                v_flagz = v_kqTe[NXT - 1]
                t_m1g = []
                v_m1e = []
                sg_m1 = []
                for grp in range(5):
                    i0, i1 = 4 * grp, min(4 * grp + 4, NXT)

                    @block.tensor
                    def _(tensor, grp=grp, i0=i0, i1=i1):
                        if grp == 0:
                            tensor.wait_ge(VC.sem, v_flagz)
                        else:
                            tensor.wait_ge(VC.sem, v_m1e[grp - 1])
                        for i in range(i0, i1):
                            mm = tensor.transpose(ps[6][0:1, 128 * (i - i0):
                                                        128 * (i - i0) + 128],
                                                  m1sc[:, i:i + 1], ident[:])
                        t_m1g.append(TC.tagw(tensor, mm))

                    @block.vector
                    def _(vector, grp=grp, i0=i0, i1=i1):
                        vector.wait_ge(TC.sem, t_m1g[grp])
                        if grp >= 1:
                            vector.wait_ge(SG.sem, sg_m1[grp - 1])
                        VC.tagw(vector, vector.tensor_copy(
                            m1stg[0:1, 0:128 * (i1 - i0)],
                            ps[6][0:1, 0:128 * (i1 - i0)]))
                        v_m1e.append(VC.n)

                    @block.gpsimd
                    def _(gpsimd, grp=grp, i0=i0, i1=i1):
                        gpsimd.wait_ge(VC.sem, v_m1e[grp])
                        sg_m1.append(SG.tag(gpsimd.dma_start(
                            out=m1d[0:1, 128 * i0:128 * i1],
                            in_=m1stg[0:1, 0:128 * (i1 - i0)])))

                sg_m1row = SG.n

                barrier()

            with ExitStack() as pfs:
                ef = pfs.enter_context
                kw = [ef(nc.sbuf_tensor(f"kw{i}", [128, 8 * 258], BF16))
                      for i in range(2)]
                ybr = [ef(nc.sbuf_tensor(f"ybr{i}", [128, FBLK], F32))
                       for i in range(4)]
                ybq = [ef(nc.sbuf_tensor(f"ybq{i}", [128, FBLK], F32))
                       for i in range(4)]
                k2b = [ef(nc.sbuf_tensor(f"k2b{i}", [128, FBLK], BF16))
                       for i in range(4)]
                repm = ef(nc.sbuf_tensor("repm", [128, FBLK + 2], F32))
                h1s = [ef(nc.sbuf_tensor(f"h1s{i}", [128, FBLK + 2], F32))
                       for i in range(2)]
                cv0 = [ef(nc.sbuf_tensor(f"cv0_{i}", [128, FBLK], F32))
                       for i in range(4)]
                cv1 = [ef(nc.sbuf_tensor(f"cv1_{i}", [128, FBLK], F32))
                       for i in range(2)]
                ymax2 = [ef(nc.sbuf_tensor(f"ymax{i}", [128, FBLK], F32))
                         for i in range(2)]
                sc2r = ef(nc.sbuf_tensor("sc2r", [1, FBLK], F32))
                xs2 = [ef(nc.sbuf_tensor(f"xs2_{i}", [128, FBLK], F32))
                       for i in range(2)]
                k2i = [ef(nc.sbuf_tensor(f"k2i_{i}", [128, FBLK], I32))
                       for i in range(2)]
                k2f = [ef(nc.sbuf_tensor(f"k2f_{i}", [128, FBLK], F32))
                       for i in range(2)]
                n2b = ef(nc.sbuf_tensor("n2b", [128, DIM], F32))
                m1fb = [ef(nc.sbuf_tensor(f"m1fb{i}", [1, FBLK + 2], F32))
                        for i in range(2)]
                yabs = xs2
                c2t = cv0
                m2t = cv1

                ld_kw = [None] * (NFB + 1)
                LD_M1 = [None] * NFB
                sg_x2 = [None] * (2 * NFB)
                t_h1_last = [None] * NFB
                NPH = 16
                for fb in range(NFB):
                    w0 = 127 + FBLK * fb
                    r0 = 128 + FBLK * fb

                    @block.sync
                    def _(sync, fb=fb, w0=w0):
                        if fb == 0:
                            sync.wait_ge(SG.sem, sg_kq[NXT - 1])
                            sync.wait_ge(SG.sem, sg_m1row)
                        if fb >= 2:
                            sync.wait_ge(TC.sem, t_h1_last[fb - 2])
                        LD.tag(sync.dma_start(out=m1fb[fb % 2][:],
                                              in_=m1d[0:1, w0:w0 + FBLK + 2]))
                        for f in range(8):
                            LD.tag(sync.dma_start(
                                out=kw[fb % 2][:, 258 * f:258 * (f + 1)],
                                in_=kqd[128 * f:128 * (f + 1), w0:w0 + FBLK + 2]))

                    ld_kwf = LD.n
                    ld_kw[fb] = ld_kwf
                    LD_M1[fb] = ld_kwf

                    @block.tensor
                    def _(tensor, fb=fb, ld_kwf=None):
                        if fb == 0:
                            tensor.wait_ge(LD.sem, ld_tw)
                        tensor.wait_ge(LD.sem, LD_M1[fb])
                        TC.tagw(tensor, tensor.matmul(ps[3][:, 0:FBLK + 2], ones_r[:],
                                             m1fb[fb % 2][:],
                                             start=True, stop=True))

                    t_rep = TC.n

                    @block.vector
                    def _(vector, t_rep=t_rep):
                        vector.wait_ge(TC.sem, t_rep)
                        VC.tagw(vector, vector.tensor_copy(repm[:], ps[3][:, 0:FBLK + 2]))

                    v_repm = VC.n

                    a_yp = [None] * NPH
                    t_h1p = [None] * NPH
                    v_h1p = [None] * NPH
                    v_ymp = [None] * NPH
                    sg_yp = [None] * NPH
                    for ph in range(NPH):
                        h0, h1_ = 2 * ph, 2 * ph + 1

                        @block.tensor
                        def _(tensor, ph=ph, h0=h0, h1_=h1_, fb=fb,
                              v_repm=v_repm, ld_kwf=ld_kwf):
                            if ph < 3:
                                tensor.wait_ge(VC.sem, v_repm)
                            if ph == 0:
                                tensor.wait_ge(LD.sem, ld_kwf)
                            if ph >= 3:
                                tensor.wait_ge(VC.sem, v_h1p[ph - 3])
                            for hc in (h0, h1_):
                                for k in range(8):
                                    mm = tensor.matmul(
                                        ps[hc % 6][:, 0:FBLK + 2],
                                        t1T[k][:, 128 * hc:128 * (hc + 1)],
                                        kw[fb % 2][:, 258 * k:258 * (k + 1)],
                                        start=(k == 0), stop=(k == 7))
                            t_h1p[ph] = TC.tagw(tensor, mm)
                            if ph == NPH - 1:
                                t_h1_last[fb] = t_h1p[ph]

                        @block.vector
                        def _(vector, ph=ph, h0=h0, h1_=h1_):
                            vector.wait_ge(TC.sem, t_h1p[ph])
                            if ph >= 2:
                                vector.wait_ge(AC.sem, a_yp[ph - 2])
                            cA, cB = cv0[h0 % 4], cv0[h1_ % 4]
                            vector.tensor_mul(h1s[0][:], ps[h0 % 6][:, 0:FBLK + 2],
                                              repm[:])
                            vector.tensor_mul(h1s[1][:], ps[h1_ % 6][:, 0:FBLK + 2],
                                              repm[:])
                            vector.tensor_scalar_mul(cA[:], h1s[0][:, 0:FBLK],
                                                     cw_sb[:, 3 * h0:3 * h0 + 1])
                            vector.tensor_scalar_mul(cB[:], h1s[1][:, 0:FBLK],
                                                     cw_sb[:, 3 * h1_:3 * h1_ + 1])
                            vector.scalar_tensor_tensor(
                                cv1[0][:], h1s[0][:, 1:FBLK + 1],
                                cw_sb[:, 3 * h0 + 1:3 * h0 + 2],
                                cA[:], ALU.mult, ALU.add)
                            vector.scalar_tensor_tensor(
                                cv1[1][:], h1s[1][:, 1:FBLK + 1],
                                cw_sb[:, 3 * h1_ + 1:3 * h1_ + 2],
                                cB[:], ALU.mult, ALU.add)
                            vector.scalar_tensor_tensor(
                                cA[:], h1s[0][:, 2:FBLK + 2],
                                cw_sb[:, 3 * h0 + 2:3 * h0 + 3],
                                cv1[0][:], ALU.mult, ALU.add)
                            v_h1p[ph] = VC.tagw(vector, vector.scalar_tensor_tensor(
                                cB[:], h1s[1][:, 2:FBLK + 2],
                                cw_sb[:, 3 * h1_ + 2:3 * h1_ + 3],
                                cv1[1][:], ALU.mult, ALU.add))

                        @block.scalar
                        def _(scalar, ph=ph, h0=h0, h1_=h1_):
                            scalar.wait_ge(VC.sem, v_h1p[ph])
                            if ph >= 2:
                                scalar.wait_ge(SG.sem, sg_yp[ph - 2])
                            if ph >= 1:
                                scalar.wait_ge(VC.sem, v_ymp[ph - 1])
                            scalar.activation(ybr[h0 % 4][:], cv0[h0 % 4][:],
                                              AF.Silu, bias=cb_sb[:, h0:h0 + 1])
                            scalar.activation(ybr[h1_ % 4][:], cv0[h1_ % 4][:],
                                              AF.Silu, bias=cb_sb[:, h1_:h1_ + 1])
                            scalar.activation(yabs[0][:], ybr[h0 % 4][:], AF.Abs)
                            a_yp[ph] = AC.tagw(scalar, scalar.activation(
                                yabs[1][:], ybr[h1_ % 4][:], AF.Abs))

                        @block.vector
                        def _(vector, ph=ph):
                            vector.wait_ge(AC.sem, a_yp[ph])
                            if ph == 0:
                                vector.tensor_copy(ymax2[0][:], yabs[0][:])
                                mm = vector.tensor_copy(ymax2[1][:], yabs[1][:])
                            else:
                                vector.tensor_max(ymax2[0][:], ymax2[0][:],
                                                  yabs[0][:])
                                mm = vector.tensor_max(ymax2[1][:], ymax2[1][:],
                                                       yabs[1][:])
                            v_ymp[ph] = VC.tagw(vector, mm)

                        @block.gpsimd
                        def _(gpsimd, ph=ph, h0=h0, h1_=h1_, fb=fb):
                            gpsimd.wait_ge(VC.sem, v_ymp[ph])
                            SG.tag(gpsimd.dma_start(
                                out=yd[32 * fb + h0:32 * fb + h0 + 1, :]
                                    .rearrange("o (p c) -> (o p) c", p=128),
                                in_=ybr[h0 % 4][:]))
                            sg_yp[ph] = SG.tag(gpsimd.dma_start(
                                out=yd[32 * fb + h1_:32 * fb + h1_ + 1, :]
                                    .rearrange("o (p c) -> (o p) c", p=128),
                                in_=ybr[h1_ % 4][:]))

                    @block.tensor
                    def _(tensor):
                        tensor.wait_ge(VC.sem, v_ymp[NPH - 1])
                        tensor.transpose(ps[0][:, 0:128], ymax2[0][:, 0:128],
                                         ident[:])
                        tensor.transpose(ps[0][:, 128:256], ymax2[0][:, 128:256],
                                         ident[:])
                        tensor.transpose(ps[0][:, 256:384], ymax2[1][:, 0:128],
                                         ident[:])
                        TC.tagw(tensor, tensor.transpose(
                            ps[0][:, 384:512], ymax2[1][:, 128:256], ident[:]))

                    t_ymT = TC.n

                    @block.vector
                    def _(vector, t_ymT=t_ymT):
                        vector.wait_ge(TC.sem, t_ymT)
                        vector.tensor_reduce(st3[:, 0:1], ps[0][:, 0:128],
                                             AX.X, ALU.max)
                        vector.tensor_reduce(st3[:, 4:5], ps[0][:, 256:384],
                                             AX.X, ALU.max)
                        vector.tensor_reduce(st3[:, 1:2], ps[0][:, 128:256],
                                             AX.X, ALU.max)
                        vector.tensor_reduce(st3[:, 5:6], ps[0][:, 384:512],
                                             AX.X, ALU.max)
                        vector.tensor_max(st3[:, 0:1], st3[:, 0:1], st3[:, 4:5])
                        vsw(vector, vector.tensor_max(st3[:, 1:2], st3[:, 1:2],
                                                      st3[:, 5:6]))
                        vector.tensor_scalar(st3[:, 2:4], st3[:, 0:2], 1e-5,
                                             1.0 / 127.0, ALU.max, ALU.mult)
                        vsw(vector, vector.tensor_scalar(
                            st3[:, 0:2], st3[:, 0:2], 1e-5, m127[:, 1:2],
                            ALU.max, ALU.mult))
                        vector.reciprocal(st3[:, 6:7], st3[:, 2:3])
                        VC.tagw(vector, vector.reciprocal(st3[:, 7:8],
                                                          st3[:, 3:4]))

                    v_sc2 = VC.n

                    @block.tensor
                    def _(tensor, v_sc2=v_sc2):
                        tensor.wait_ge(VC.sem, v_sc2)
                        tensor.transpose(ps[1][0:1, 0:128], st3[:, 6:7], ident[:])
                        TC.tagw(tensor, tensor.transpose(ps[1][0:1, 128:256],
                                                st3[:, 7:8], ident[:]))

                    t_s2T = TC.n

                    @block.vector
                    def _(vector, t_s2T=t_s2T):
                        vector.wait_ge(TC.sem, t_s2T)
                        VC.tagw(vector, vector.tensor_copy(sc2r[:], ps[1][0:1, 0:FBLK]))

                    v_s2r = VC.n

                    @block.tensor
                    def _(tensor, v_s2r=v_s2r):
                        tensor.wait_ge(VC.sem, v_s2r)
                        TC.tagw(tensor, tensor.matmul(ps[3][:, 0:FBLK], ones_r[:], sc2r[:],
                                             start=True, stop=True))

                    t_rp2 = TC.n

                    v_k2p = [None] * NPH
                    t_mm2p = [None] * NPH
                    ld_yqp = [None] * NPH
                    for ph in range(NPH):
                        h0, h1_ = 2 * ph, 2 * ph + 1

                        @block.sync
                        def _(sync, ph=ph, h0=h0, h1_=h1_, fb=fb):
                            sync.wait_ge(SG.sem, sg_yp[ph])
                            if ph >= 2:
                                sync.wait_ge(VC.sem, v_k2p[ph - 2])
                            LD.tag(sync.dma_start(
                                out=ybq[h0 % 4][:],
                                in_=yd[32 * fb + h0:32 * fb + h0 + 1, :]
                                    .rearrange("o (p c) -> (o p) c", p=128)))
                            ld_yqp[ph] = LD.tag(sync.dma_start(
                                out=ybq[h1_ % 4][:],
                                in_=yd[32 * fb + h1_:32 * fb + h1_ + 1, :]
                                    .rearrange("o (p c) -> (o p) c", p=128)))

                        @block.vector
                        def _(vector, ph=ph, h0=h0, h1_=h1_, t_rp2=t_rp2):
                            vector.wait_ge(LD.sem, ld_yqp[ph])
                            if ph == 0:
                                vector.wait_ge(TC.sem, t_rp2)
                            if ph >= 2:
                                vector.wait_ge(TC.sem, t_mm2p[ph - 2])
                            vector.tensor_mul(xs2[0][:], ybq[h0 % 4][:],
                                              ps[3][:, 0:FBLK])
                            vector.tensor_mul(xs2[1][:], ybq[h1_ % 4][:],
                                              ps[3][:, 0:FBLK])
                            vector.tensor_copy(k2i[0][:], xs2[0][:])
                            vector.tensor_copy(k2i[1][:], xs2[1][:])
                            vector.tensor_copy(k2f[0][:], k2i[0][:])
                            vector.tensor_copy(k2f[1][:], k2i[1][:])
                            vector.tensor_sub(c2t[0][:], xs2[0][:], k2f[0][:])
                            vector.tensor_sub(c2t[1][:], xs2[1][:], k2f[1][:])
                            vector.tensor_single_scalar(m2t[0][:], c2t[0][:],
                                                        -0.5, ALU.is_le)
                            vector.tensor_single_scalar(m2t[1][:], c2t[1][:],
                                                        -0.5, ALU.is_le)
                            vector.scalar_tensor_tensor(c2t[0][:], c2t[0][:], 0.5,
                                                        k2f[0][:], ALU.is_ge,
                                                        ALU.add)
                            vector.scalar_tensor_tensor(c2t[1][:], c2t[1][:], 0.5,
                                                        k2f[1][:], ALU.is_ge,
                                                        ALU.add)
                            vector.tensor_sub(k2b[h0 % 4][:], c2t[0][:], m2t[0][:])
                            v_k2p[ph] = VC.tagw(vector, vector.tensor_sub(
                                k2b[h1_ % 4][:], c2t[1][:], m2t[1][:]))

                        @block.tensor
                        def _(tensor, ph=ph, h0=h0, h1_=h1_):
                            tensor.wait_ge(VC.sem, v_k2p[ph])
                            for hc in (h0, h1_):
                                for h in range(2):
                                    for s in range(2):
                                        mm = tensor.matmul(
                                            ps[4 + 2 * h + s][:, 0:512],
                                            k2b[hc % 4][:, 128 * h:128 * (h + 1)],
                                            t2T[hc][:, 512 * s:512 * (s + 1)],
                                            start=(hc == 0), stop=(hc == 31))
                            t_mm2p[ph] = TC.tagw(tensor, mm)

                    for h in range(2):
                        @block.sync
                        def _(sync, h=h, r0=r0, fb=fb):
                            sync.wait_ge(TC.sem, t_mm2p[NPH - 1])
                            if fb > 0:
                                sync.wait_ge(SG.sem, sg_x2[(fb - 1) * 2 + h])
                            if h == 1:
                                sync.wait_ge(SG.sem, sg_x2[fb * 2])
                            LD.tag(sync.dma_start(
                                out=x1b[h][:],
                                in_=x1d[r0 + 128 * h:r0 + 128 * (h + 1), :]))

                        ld_x1 = LD.n

                        @block.vector
                        def _(vector, h=h, ld_x1=ld_x1):
                            vector.wait_ge(LD.sem, ld_x1)
                            for s in range(2):
                                vector.tensor_mul(
                                    n2b[:, 512 * s:512 * (s + 1)],
                                    ps[4 + 2 * h + s][:, 0:512],
                                    gb_rep[:, 512 * s:512 * (s + 1)])
                            vector.scalar_tensor_tensor(
                                x1b[h][:, 0:512], n2b[:, 0:512],
                                st3[:, h:h + 1], x1b[h][:, 0:512],
                                ALU.mult, ALU.add)
                            VC.tagw(vector, vector.scalar_tensor_tensor(
                                x1b[h][:, 512:1024], n2b[:, 512:1024],
                                st3[:, h:h + 1], x1b[h][:, 512:1024],
                                ALU.mult, ALU.add))

                        v_x2 = VC.n

                        @block.gpsimd
                        def _(gpsimd, h=h, fb=fb, v_x2=v_x2):
                            gpsimd.wait_ge(VC.sem, v_x2)
                            sg_x2[fb * 2 + h] = SG.tag(gpsimd.dma_start(
                                out=outd[FBLK * fb + 128 * h:
                                         FBLK * fb + 128 * (h + 1), :],
                                in_=x1b[h][:]))

            @block.sync
            def _(sync):
                sync.wait_ge(SG.sem, SG.n)
                sync.wait_ge(LD.sem, LD.n)

    return nc


_NC_CACHE = None


def kernel(**inputs):
    global _NC_CACHE
    x = np.ascontiguousarray(np.asarray(inputs["L0_attn_x"], dtype=np.float32)
                             .reshape(16384, DIM))
    wqkv = np.ascontiguousarray(np.asarray(inputs["L0_attn_W_qkv"], np.float32))
    wout = np.ascontiguousarray(np.asarray(inputs["L0_attn_W_out"], np.float32))
    w1 = np.ascontiguousarray(np.asarray(inputs["L0_ffn_w1"], np.float32))
    convw = np.asarray(inputs["L0_ffn_conv_w"], np.float32)
    convb = np.asarray(inputs["L0_ffn_conv_b"], np.float32)
    w2 = np.ascontiguousarray(np.asarray(inputs["L0_ffn_w2"], np.float32))
    cvec = np.ascontiguousarray(np.asarray(inputs["L0_c"], np.float32)
                                .reshape(1, DIM))
    wmod = np.ascontiguousarray(np.asarray(inputs["L0_wmod"], np.float32))

    cw = convw[:, 0, :].reshape(32, 128, 3).transpose(1, 0, 2).reshape(128, 96)
    cb = convb.reshape(32, 128).T
    cw = np.ascontiguousarray(cw)
    cb = np.ascontiguousarray(cb)

    in_maps = []
    for j in range(8):
        t0 = 2048 * j
        xexts = np.zeros((ATT, DIM), np.float32)
        l0 = max(t0 - 128, 0)
        xexts[128 - (t0 - l0):128] = x[l0:t0]
        xexts[128:128 + 2048] = x[t0:t0 + 2048]
        xr = np.zeros((1, DIM), np.float32)
        if j < 7:
            xr[0] = x[t0 + 2048]
        xpres = np.zeros((PRE, DIM), np.float32)
        npr = max(t0 - 128, 0)
        xpres[0:npr] = x[0:npr]
        in_maps.append({
            "xext": xexts,
            "xr1": xr,
            "xpre": xpres,
            "npre": np.array([[float(npr)]], np.float32),
            "haslt": np.array([[1.0 if j > 0 else 0.0]], np.float32),
            "hasrt": np.array([[1.0 if j < 7 else 0.0]], np.float32),
            "wqkv": wqkv,
            "woutd": wout,
            "w1": w1,
            "w2": w2,
            "cvec": cvec,
            "wmod": wmod,
            "convw": cw,
            "convb": cb,
        })

    if _NC_CACHE is None:
        _NC_CACHE = build_nc()
    res = run_bass_kernel_spmd(_NC_CACHE, in_maps, core_ids=list(range(8)))
    out = np.concatenate([r["outd"] for r in res.results], axis=0)
    return out.reshape(1, 16384, DIM)


# version 16
# speedup vs baseline: 1.0259x; 1.0259x over previous
import numpy as np
from contextlib import ExitStack

import concourse.bass as bass
import concourse.mybir as mybir
from concourse.bass_utils import run_bass_kernel_spmd

F32 = mybir.dt.float32
BF16 = mybir.dt.bfloat16
I32 = mybir.dt.int32
ALU = mybir.AluOpType
AF = mybir.ActivationFunctionType
AX = mybir.AxisListType

T = 2048
ATT = 2176
X1R = 2304
PRE = 14208
DIM = 1024
DH = 4096
EPS = 1e-5
SOFTB = 30.0
SCALE = 0.125
CSCALE = 0.125 / 64.0
VSCALE = 0.5 / 64.0
ABLK = [(0, 128), (128, 512), (640, 512), (1152, 512), (1664, 512)]
FBLK = 256
NFB = T // FBLK


class Ctr:
    def __init__(self, sem, step=1):
        self.sem, self.step, self.n = sem, step, 0

    def tag(self, inst):
        inst.then_inc(self.sem, self.step)
        self.n += self.step
        return self.n

    def tagw(self, eng, inst):
        return self.tag(inst)


def build_nc():
    nc = bass.Bass("TRN2", target_bir_lowering=False, debug=False,
                   detect_race_conditions=False)

    xext = nc.dram_tensor("xext", [ATT, DIM], F32, kind="ExternalInput").ap()
    xr1 = nc.dram_tensor("xr1", [1, DIM], F32, kind="ExternalInput").ap()
    xpre = nc.dram_tensor("xpre", [PRE, DIM], F32, kind="ExternalInput").ap()
    npre = nc.dram_tensor("npre", [1, 1], F32, kind="ExternalInput").ap()
    haslt = nc.dram_tensor("haslt", [1, 1], F32, kind="ExternalInput").ap()
    hasrt = nc.dram_tensor("hasrt", [1, 1], F32, kind="ExternalInput").ap()
    wqkv = nc.dram_tensor("wqkv", [DIM, 3 * DIM], F32, kind="ExternalInput").ap()
    woutd = nc.dram_tensor("woutd", [DIM, DIM], F32, kind="ExternalInput").ap()
    w1 = nc.dram_tensor("w1", [DH, DIM], F32, kind="ExternalInput").ap()
    w2 = nc.dram_tensor("w2", [DIM, DH], F32, kind="ExternalInput").ap()
    cvec = nc.dram_tensor("cvec", [1, DIM], F32, kind="ExternalInput").ap()
    wmod = nc.dram_tensor("wmod", [DIM, 6 * DIM], F32, kind="ExternalInput").ap()
    convw = nc.dram_tensor("convw", [128, 96], F32, kind="ExternalInput").ap()
    convb = nc.dram_tensor("convb", [128, 32], F32, kind="ExternalInput").ap()
    outd = nc.dram_tensor("outd", [T, DIM], F32, kind="ExternalOutput").ap()
    x1d = nc.dram_tensor("x1d", [X1R, DIM], F32, kind="Internal").ap()
    t1d = nc.dram_tensor("t1d", [DH, DIM], BF16, kind="Internal").ap()
    t2d = nc.dram_tensor("t2d", [DIM, DH], BF16, kind="Internal").ap()
    kqd = nc.dram_tensor("kqd", [8 * 128, X1R], BF16, kind="Internal").ap()
    yd = nc.dram_tensor("yd", [NFB * 32, 128 * FBLK], F32, kind="Internal").ap()
    m1d = nc.dram_tensor("m1d", [1, X1R], F32, kind="Internal").ap()

    ctx = ExitStack()
    with ctx:
        e = ctx.enter_context

        LD = Ctr(e(nc.semaphore("sLD")), 16)
        SG = Ctr(e(nc.semaphore("sSG")), 16)
        TC = Ctr(e(nc.semaphore("sTC")))
        VC = Ctr(e(nc.semaphore("sVC")))
        AC = Ctr(e(nc.semaphore("sAC")))
        GC = Ctr(e(nc.semaphore("sGC")))

        ident = e(nc.sbuf_tensor("ident", [128, 128], F32))
        ident1 = e(nc.sbuf_tensor("ident1", [1, 1], F32))
        cst_a = e(nc.sbuf_tensor("cst_a", [128, 1], F32))
        cst_b = e(nc.sbuf_tensor("cst_b", [128, 1], F32))
        cst_c = e(nc.sbuf_tensor("cst_c", [128, 1], F32))
        cst_d = e(nc.sbuf_tensor("cst_d", [128, 1], F32))
        identb = e(nc.sbuf_tensor("identb", [128, 128], BF16))
        ones_c = e(nc.sbuf_tensor("ones_c", [128, 1], F32))
        ones_r = e(nc.sbuf_tensor("ones_r", [1, 128], F32))
        onesE = e(nc.sbuf_tensor("onesE", [64, 33], F32))
        onesO = e(nc.sbuf_tensor("onesO", [64, 33], F32))
        ones11 = e(nc.sbuf_tensor("ones11", [1, 11], F32))
        onesVs = [e(nc.sbuf_tensor(f"onesVs{c}", [64, 10], F32)) for c in range(10)]
        halfsel = e(nc.sbuf_tensor("halfsel", [33, 128], F32))
        parmask = e(nc.sbuf_tensor("parmask", [128, 2], F32))
        tri = e(nc.sbuf_tensor("tri", [10, 11], F32))
        tmp_row = e(nc.sbuf_tensor("tmp_row", [1, DIM], F32))
        sc_cols = e(nc.sbuf_tensor("sc_cols", [128, 8], F32))
        ca1_cols = e(nc.sbuf_tensor("ca1_cols", [128, 8], F32))
        sa_cols = e(nc.sbuf_tensor("sa_cols", [128, 8], F32))
        cb1_rep = e(nc.sbuf_tensor("cb1_rep", [128, DIM], F32))
        sb_rep = e(nc.sbuf_tensor("sb_rep", [128, DIM], F32))
        gb_rep = e(nc.sbuf_tensor("gb_rep", [128, DIM], F32))
        ga_row = e(nc.sbuf_tensor("ga_row", [1, DIM], F32))
        flags = e(nc.sbuf_tensor("flags", [1, 4], F32))
        haslc = e(nc.sbuf_tensor("haslc", [128, 1], F32))
        hasrc = e(nc.sbuf_tensor("hasrc", [128, 1], F32))
        cw_sb = e(nc.sbuf_tensor("cw_sb", [128, 96], F32))
        cb_sb = e(nc.sbuf_tensor("cb_sb", [128, 32], F32))
        wsc = e(nc.sbuf_tensor("wsc", [1, 8], F32))
        wsc_col = e(nc.sbuf_tensor("wsc_col", [128, 2], F32))
        m_col = e(nc.sbuf_tensor("m_col", [128, 2], F32))
        m127 = e(nc.sbuf_tensor("m127", [128, 2], F32))
        hmodT = e(nc.sbuf_tensor("hmodT", [128, 8], F32))
        basek = e(nc.sbuf_tensor("basek", [128, 8], F32))
        carryk = e(nc.sbuf_tensor("carryk", [128, 8], F32))
        hr_cols = e(nc.sbuf_tensor("hr_cols", [128, 8], F32))
        qr_sb = e(nc.sbuf_tensor("qr_sb", [128, 8], F32))
        vr_sb = e(nc.sbuf_tensor("vr_sb", [128, 8], F32))
        xr_row = e(nc.sbuf_tensor("xr_row", [1, DIM], F32))

        ps = [e(nc.psum_tensor(f"ps{i}", [128, 512], F32)) for i in range(8)]

        nc.const_aps.register(F32, -SOFTB, cst_a[:])
        nc.const_aps.register(F32, SCALE, cst_b[:])
        nc.const_aps.register(F32, CSCALE, cst_c[:])
        nc.const_aps.register(F32, EPS, cst_d[:])

        block = e(nc.Block())

        p01 = ExitStack()
        c_row = p01.enter_context(nc.sbuf_tensor("c_row", [1, DIM], F32))
        mod_row = p01.enter_context(nc.sbuf_tensor("mod_row", [1, 6 * DIM], F32))
        hsum_row = p01.enter_context(nc.sbuf_tensor("hsum_row", [1, DIM], F32))

        def vsw(vector, inst):
            VC.tag(inst)
            vector.wait_ge(VC.sem, VC.n)

        def barrier():
            tn, vn, an, gn, ldn, sgn = TC.n, VC.n, AC.n, GC.n, LD.n, SG.n

            def wait_all(eng):
                eng.wait_ge(TC.sem, tn)
                eng.wait_ge(VC.sem, vn)
                eng.wait_ge(AC.sem, an)
                eng.wait_ge(GC.sem, gn)
                eng.wait_ge(LD.sem, ldn)
                eng.wait_ge(SG.sem, sgn)

            for emit in (block.tensor, block.vector, block.scalar,
                         block.gpsimd, block.sync):
                @emit
                def _(eng):
                    wait_all(eng)

        def ln_stats(src, st, ascr, rows=128, wait_ld=None, wait_v=None):
            @block.scalar
            def _(scalar):
                if wait_ld is not None:
                    scalar.wait_ge(LD.sem, wait_ld)
                if wait_v is not None:
                    scalar.wait_ge(VC.sem, wait_v)
                scalar.activation(ascr[0:rows, :], src[0:rows, :], AF.Identity,
                                  accum_out=st[0:rows, 0:1])
                AC.tagw(scalar, scalar.activation(ascr[0:rows, :], src[0:rows, :], AF.Square,
                                         accum_out=st[0:rows, 1:2]))

            a1 = AC.n

            @block.vector
            def _(vector):
                vector.wait_ge(AC.sem, a1)
                vector.tensor_single_scalar(st[0:rows, 2:3], st[0:rows, 0:1],
                                            1.0 / DIM, ALU.mult)
                vector.tensor_single_scalar(st[0:rows, 3:4], st[0:rows, 1:2],
                                            1.0 / DIM, ALU.mult)
                vsw(vector, vector.tensor_mul(st[0:rows, 4:5], st[0:rows, 2:3],
                                              st[0:rows, 2:3]))
                VC.tagw(vector, vector.tensor_sub(st[0:rows, 5:6], st[0:rows, 3:4],
                                                  st[0:rows, 4:5]))

            v1 = VC.n

            @block.scalar
            def _(scalar):
                scalar.wait_ge(VC.sem, v1)
                AC.tagw(scalar, scalar.activation(st[0:rows, 6:7], st[0:rows, 5:6],
                                                  AF.Sqrt, bias=EPS))

            a2 = AC.n

            @block.vector
            def _(vector):
                vector.wait_ge(AC.sem, a2)
                vsw(vector, vector.reciprocal(st[0:rows, 7:8], st[0:rows, 6:7]))
                VC.tagw(vector, vector.engine_nop())

            return VC.n

        @block.sync
        def _(sync):
            LD.tag(sync.dma_start(out=c_row[:], in_=cvec))
            LD.tag(sync.dma_start(out=cw_sb[:], in_=convw))
            LD.tag(sync.dma_start(out=cb_sb[:], in_=convb))
            LD.tag(sync.dma_start(out=flags[0:1, 0:1], in_=npre))
            LD.tag(sync.dma_start(out=flags[0:1, 1:2], in_=haslt))
            LD.tag(sync.dma_start(out=flags[0:1, 2:3], in_=hasrt))
            LD.tag(sync.dma_start(out=xr_row[:], in_=xr1))

        ld_p0 = LD.n

        @block.gpsimd
        def _(gpsimd):
            def lvl():
                gpsimd.wait_ge(GC.sem, GC.n)

            t = GC.tag
            t(gpsimd.memset(ident[:], 0.0))
            t(gpsimd.memset(identb[:], 0.0))
            t(gpsimd.memset(ident1[:], 1.0))
            t(gpsimd.memset(cst_a[:], -SOFTB))
            t(gpsimd.memset(cst_b[:], SCALE))
            t(gpsimd.memset(cst_c[:], CSCALE))
            t(gpsimd.memset(cst_d[:], EPS))
            t(gpsimd.memset(ones_c[:], 1.0))
            t(gpsimd.memset(ones_r[:], 1.0))
            t(gpsimd.memset(onesE[:], 0.0))
            t(gpsimd.memset(onesO[:], 0.0))
            t(gpsimd.memset(ones11[:], 1.0))
            for c in range(10):
                t(gpsimd.memset(onesVs[c][:], 0.0))
            t(gpsimd.memset(halfsel[:], 0.0))
            t(gpsimd.memset(parmask[:], 0.0))
            t(gpsimd.memset(tri[:], 1.0))
            t(gpsimd.memset(carryk[:], 0.0))
            t(gpsimd.memset(basek[:], 0.0))
            t(gpsimd.memset(qr_sb[:], 0.0))
            t(gpsimd.memset(vr_sb[:], 0.0))
            lvl()
            t(gpsimd.affine_select(out=ident[:], in_=ident[:],
                                   compare_op=ALU.not_equal,
                                   fill=1.0, base=0, pattern=[[-1, 128]],
                                   channel_multiplier=1))
            t(gpsimd.affine_select(out=identb[:], in_=identb[:],
                                   compare_op=ALU.not_equal,
                                   fill=1.0, base=0, pattern=[[-1, 128]],
                                   channel_multiplier=1))
            t(gpsimd.memset(onesE[0:64, 0:1], 1.0))
            t(gpsimd.memset(onesO[0:64, 32:33], 1.0))
            for c in range(10):
                t(gpsimd.memset(onesVs[c][0:64, c:c + 1], 1.0))
            t(gpsimd.memset(halfsel[0:1, 0:64], 1.0))
            t(gpsimd.memset(halfsel[32:33, 64:128], 1.0))
            t(gpsimd.memset(parmask[0:64, 0:1], 1.0))
            t(gpsimd.memset(parmask[64:128, 1:2], 1.0))
            t(gpsimd.affine_select(out=tri[:], in_=tri[:],
                                   compare_op=ALU.is_ge, fill=0.0, base=-2,
                                   pattern=[[1, 11]], channel_multiplier=-1))
            lvl()
            GC.tagw(gpsimd, gpsimd.memset(tri[:, 0:1], 1.0))

        g_consts = GC.n

        @block.tensor
        def _(tensor):
            tensor.wait_ge(GC.sem, g_consts)
            tensor.wait_ge(LD.sem, ld_p0)
            tensor.matmul(ps[7][:, 0:1], ones_r[:], flags[0:1, 1:2],
                          start=True, stop=True)
            TC.tagw(tensor, tensor.matmul(ps[7][:, 1:2], ones_r[:], flags[0:1, 2:3],
                                 start=True, stop=True))

        t_flags = TC.n

        @block.vector
        def _(vector):
            vector.wait_ge(TC.sem, t_flags)
            vector.tensor_copy(haslc[:], ps[7][:, 0:1])
            VC.tagw(vector, vector.tensor_copy(hasrc[:], ps[7][:, 1:2]))

        @block.scalar
        def _(scalar):
            scalar.wait_ge(LD.sem, ld_p0)
            AC.tagw(scalar, scalar.activation(c_row[:], c_row[:], AF.Silu))

        a_siluc = AC.n

        @block.tensor
        def _(tensor):
            tensor.wait_ge(AC.sem, a_siluc)
            for f in range(8):
                mm = tensor.transpose(ps[0][:, f:f + 1],
                                      c_row[0:1, 128 * f:128 * (f + 1)],
                                      ident1[:])
            TC.tagw(tensor, mm)

        t_sct = TC.n

        @block.vector
        def _(vector):
            vector.wait_ge(TC.sem, t_sct)
            VC.tagw(vector, vector.tensor_copy(sc_cols[:], ps[0][:, 0:8]))

        v_sccols = VC.n

        with ExitStack() as p0s:
            wm_buf = [p0s.enter_context(nc.sbuf_tensor(f"wm{i}", [128, 6 * DIM], F32))
                      for i in range(2)]
            wmod_ld = []
            v_modgrp = []

            for f in range(8):
                @block.sync
                def _(sync, f=f):
                    if f >= 2:
                        sync.wait_ge(VC.sem, v_modgrp[3 * (f - 2) + 2])
                    wmod_ld.append(LD.tag(sync.dma_start(
                        out=wm_buf[f % 2][:], in_=wmod[128 * f:128 * (f + 1), :])))

                t_grp = []
                for grp in range(3):
                    @block.tensor
                    def _(tensor, f=f, grp=grp):
                        if grp == 0:
                            tensor.wait_ge(LD.sem, wmod_ld[f])
                            if f == 0:
                                tensor.wait_ge(VC.sem, v_sccols)
                        if grp > 0:
                            tensor.wait_ge(VC.sem, v_modgrp[3 * f + grp - 1])
                        elif f >= 1:
                            tensor.wait_ge(VC.sem, v_modgrp[3 * (f - 1) + 2])
                        for s in range(4 * grp, 4 * grp + 4):
                            mm = tensor.matmul(
                                ps[4 + s % 4][0:1, 0:512],
                                sc_cols[:, f:f + 1],
                                wm_buf[f % 2][:, 512 * s:512 * (s + 1)],
                                start=True, stop=True)
                        t_grp.append(TC.tagw(tensor, mm))

                    @block.vector
                    def _(vector, f=f, grp=grp):
                        vector.wait_ge(TC.sem, t_grp[grp])
                        for s in range(4 * grp, 4 * grp + 4):
                            dst = mod_row[0:1, 512 * s:512 * (s + 1)]
                            if f == 0:
                                mm = vector.tensor_copy(dst,
                                                        ps[4 + s % 4][0:1, 0:512])
                            else:
                                mm = vector.tensor_add(dst, dst,
                                                       ps[4 + s % 4][0:1, 0:512])
                        v_modgrp.append(VC.tagw(vector, mm))

        barrier()

        @block.scalar
        def _(scalar):
            scalar.activation(ga_row[:], mod_row[0:1, 2 * DIM:3 * DIM], AF.Sign)
            AC.tagw(scalar, scalar.activation(c_row[:], mod_row[0:1, 2 * DIM:3 * DIM], AF.Abs))

        a_ga = AC.n

        @block.vector
        def _(vector):
            vector.wait_ge(AC.sem, a_ga)
            vsw(vector, vector.tensor_scalar_max(c_row[:], c_row[:], 0.25))
            VC.tagw(vector, vector.tensor_mul(ga_row[:], ga_row[:], c_row[:]))

        v_ga = VC.n

        @block.scalar
        def _(scalar):
            scalar.wait_ge(VC.sem, v_ga)
            scalar.activation(tmp_row[:], mod_row[0:1, 5 * DIM:6 * DIM], AF.Sign)
            AC.tagw(scalar, scalar.activation(c_row[:], mod_row[0:1, 5 * DIM:6 * DIM], AF.Abs))

        a_gb = AC.n

        @block.vector
        def _(vector):
            vector.wait_ge(AC.sem, a_gb)
            vector.tensor_scalar_max(c_row[:], c_row[:], 0.25)
            vector.tensor_scalar_add(mod_row[0:1, DIM:2 * DIM],
                                     mod_row[0:1, DIM:2 * DIM], 1.0)
            vector.tensor_mul(mod_row[0:1, 5 * DIM:6 * DIM], tmp_row[:], c_row[:])
            VC.tagw(vector, vector.tensor_scalar_add(mod_row[0:1, 4 * DIM:5 * DIM],
                                            mod_row[0:1, 4 * DIM:5 * DIM], 1.0))

        v_modrows = VC.n

        @block.tensor
        def _(tensor):
            tensor.wait_ge(VC.sem, v_modrows)
            for f in range(8):
                tensor.transpose(ps[0][:, f:f + 1],
                                 mod_row[0:1, 128 * f:128 * (f + 1)], ident1[:])
                tensor.transpose(ps[1][:, f:f + 1],
                                 mod_row[0:1, DIM + 128 * f:DIM + 128 * (f + 1)],
                                 ident1[:])
            tensor.matmul(ps[2][:, 0:512], ones_r[:],
                          mod_row[0:1, 4 * DIM:4 * DIM + 512], start=True, stop=True)
            tensor.matmul(ps[3][:, 0:512], ones_r[:],
                          mod_row[0:1, 4 * DIM + 512:5 * DIM], start=True, stop=True)
            tensor.matmul(ps[4][:, 0:512], ones_r[:],
                          mod_row[0:1, 3 * DIM:3 * DIM + 512], start=True, stop=True)
            TC.tagw(tensor, tensor.matmul(
                ps[5][:, 0:512], ones_r[:],
                mod_row[0:1, 3 * DIM + 512:4 * DIM], start=True, stop=True))

        t_reps1 = TC.n

        @block.vector
        def _(vector):
            vector.wait_ge(TC.sem, t_reps1)
            vector.tensor_copy(sa_cols[:], ps[0][:, 0:8])
            vector.tensor_copy(ca1_cols[:], ps[1][:, 0:8])
            vector.tensor_copy(cb1_rep[:, 0:512], ps[2][:, 0:512])
            vector.tensor_copy(cb1_rep[:, 512:1024], ps[3][:, 0:512])
            vector.tensor_copy(sb_rep[:, 0:512], ps[4][:, 0:512])
            VC.tagw(vector, vector.tensor_copy(sb_rep[:, 512:1024], ps[5][:, 0:512]))

        v_reps1 = VC.n

        @block.tensor
        def _(tensor):
            tensor.wait_ge(VC.sem, v_reps1)
            tensor.matmul(ps[2][:, 0:512], ones_r[:],
                          mod_row[0:1, 5 * DIM:5 * DIM + 512], start=True, stop=True)
            TC.tagw(tensor, tensor.matmul(
                ps[3][:, 0:512], ones_r[:],
                mod_row[0:1, 5 * DIM + 512:6 * DIM], start=True, stop=True))

        t_reps2 = TC.n

        @block.vector
        def _(vector):
            vector.wait_ge(TC.sem, t_reps2)
            vector.tensor_copy(gb_rep[:, 0:512], ps[2][:, 0:512])
            VC.tagw(vector, vector.tensor_copy(gb_rep[:, 512:1024], ps[3][:, 0:512]))

        barrier()

        with ExitStack() as p1s:
            e1 = p1s.enter_context
            xp = [e1(nc.sbuf_tensor(f"xp{i}", [128, DIM], F32)) for i in range(4)]
            ascr = e1(nc.sbuf_tensor("ascr", [128, DIM], BF16))
            xq = [e1(nc.sbuf_tensor(f"xq{i}", [128, DIM], F32)) for i in range(2)]
            st1p = [e1(nc.sbuf_tensor(f"st1_{i}", [128, 8], F32))
                    for i in range(4)]
            st1 = st1p[0]

            v_rn1 = ln_stats(xr_row, st1, ascr, rows=1)

            @block.vector
            def _(vector):
                VC.tagw(vector, vector.tensor_scalar(tmp_row[:], xr_row[:], st1[0:1, 2:3],
                                            st1[0:1, 7:8], ALU.subtract, ALU.mult))

            v_rn2 = VC.n

            @block.tensor
            def _(tensor):
                tensor.wait_ge(VC.sem, v_rn2)
                for f in range(8):
                    mm = tensor.transpose(ps[0][:, f:f + 1],
                                          tmp_row[0:1, 128 * f:128 * (f + 1)],
                                          ident1[:])
                TC.tagw(tensor, mm)

            t_rT = TC.n

            @block.vector
            def _(vector):
                vector.wait_ge(TC.sem, t_rT)
                vsw(vector, vector.tensor_mul(hr_cols[:], ps[0][:, 0:8],
                                              ca1_cols[:]))
                VC.tagw(vector, vector.tensor_add(hr_cols[:], hr_cols[:], sa_cols[:]))

            NPT = PRE // 128
            v_xr = [None] * NPT
            t_mm = [None] * NPT

            for i in range(NPT):
                @block.sync
                def _(sync, i=i):
                    if i >= 4:
                        sync.wait_ge(VC.sem, v_xr[i - 4])
                    LD.tag(sync.dma_start(out=xp[i % 4][:],
                                          in_=xpre[128 * i:128 * (i + 1), :]))

                ld_i = LD.n
                ln_stats(xp[i % 4], st1p[i % 4], ascr, wait_ld=ld_i,
                         wait_v=(v_rn1 if i == 0 else None))

                @block.vector
                def _(vector, i=i):
                    if i >= 2:
                        vector.wait_ge(TC.sem, t_mm[i - 2])
                    v_xr[i] = VC.tagw(vector, vector.tensor_scalar(
                        xq[i % 2][:], xp[i % 4][:], st1p[i % 4][:, 2:3],
                        st1p[i % 4][:, 7:8], ALU.subtract, ALU.mult))

                @block.tensor
                def _(tensor, i=i):
                    tensor.wait_ge(VC.sem, v_xr[i])
                    tensor.matmul(ps[1][0:1, 0:512], ones_c[:], xq[i % 2][:, 0:512],
                                  start=(i == 0), stop=(i == NPT - 1))
                    t_mm[i] = TC.tagw(tensor, tensor.matmul(
                        ps[2][0:1, 0:512], ones_c[:], xq[i % 2][:, 512:1024],
                        start=(i == 0), stop=(i == NPT - 1)))

            @block.vector
            def _(vector):
                vector.wait_ge(TC.sem, t_mm[NPT - 1])
                vector.tensor_copy(hsum_row[0:1, 0:512], ps[1][0:1, 0:512])
                vector.tensor_copy(hsum_row[0:1, 512:1024], ps[2][0:1, 0:512])
                vector.tensor_single_scalar(tmp_row[:], mod_row[0:1, 0:DIM],
                                            flags[0:1, 0:1], ALU.mult)
                vsw(vector, vector.tensor_mul(hsum_row[:], hsum_row[:],
                                              mod_row[0:1, DIM:2 * DIM]))
                VC.tagw(vector, vector.tensor_add(hsum_row[:], hsum_row[:], tmp_row[:]))

            v_hmod = VC.n

            @block.tensor
            def _(tensor):
                tensor.wait_ge(VC.sem, v_hmod)
                for f in range(8):
                    mm = tensor.transpose(ps[3][:, f:f + 1],
                                          hsum_row[0:1, 128 * f:128 * (f + 1)],
                                          ident1[:])
                TC.tagw(tensor, mm)

            t_hT = TC.n

            @block.vector
            def _(vector):
                vector.wait_ge(TC.sem, t_hT)
                VC.tagw(vector, vector.tensor_copy(hmodT[:], ps[3][:, 0:8]))

            barrier()

            wqbuf = [e1(nc.sbuf_tensor(f"wqb{i}", [128, DH], F32)) for i in range(2)]
            gl = e1(nc.sbuf_tensor("gl", [128, DH], F32))
            ll = e1(nc.sbuf_tensor("ll", [128, DH], F32))
            tb = [e1(nc.sbuf_tensor(f"tb{i}", [128, DH], BF16)) for i in range(2)]

            def emit_wq(wap, tdst, rows, cols, widx, prev_guard=None):
                ntile = rows // 128
                v_red = [None] * ntile
                for i in range(ntile):
                    @block.sync
                    def _(sync, i=i):
                        if i >= 2:
                            sync.wait_ge(VC.sem, v_red[i - 2])
                        elif prev_guard is not None:
                            sync.wait_ge(VC.sem, prev_guard[i])
                        LD.tag(sync.dma_start(out=wqbuf[i % 2][:, 0:cols],
                                              in_=wap[128 * i:128 * (i + 1), :]))

                    ldw = LD.n

                    @block.vector
                    def _(vector, i=i):
                        vector.wait_ge(LD.sem, ldw)
                        vsw(vector, vector.tensor_reduce(
                            st1[:, 0:1], wqbuf[i % 2][:, 0:cols],
                            AX.X, ALU.add, apply_absolute_value=True))
                        if i == 0:
                            v_red[i] = VC.tagw(vector, vector.tensor_copy(st1[:, 1:2],
                                                                 st1[:, 0:1]))
                        else:
                            v_red[i] = VC.tagw(vector, vector.tensor_add(
                                st1[:, 1:2], st1[:, 1:2], st1[:, 0:1]))

                @block.tensor
                def _(tensor):
                    tensor.wait_ge(VC.sem, v_red[ntile - 1])
                    TC.tagw(tensor, tensor.matmul(ps[3][0:1, 0:1], st1[:, 1:2], ones_c[:],
                                         start=True, stop=True))

                t_tot = TC.n
                b = 4 * widx

                @block.vector
                def _(vector):
                    vector.wait_ge(TC.sem, t_tot)
                    vector.tensor_single_scalar(wsc[0:1, b:b + 1], ps[3][0:1, 0:1],
                                                1.0 / (rows * cols), ALU.mult)
                    vsw(vector, vector.tensor_scalar(wsc[0:1, b + 1:b + 2],
                                                     ps[3][0:1, 0:1],
                                                     1.0 / (rows * cols), 1e-5,
                                                     ALU.mult, ALU.max))
                    vsw(vector, vector.reciprocal(wsc[0:1, b + 2:b + 3],
                                                  wsc[0:1, b + 1:b + 2]))
                    VC.tagw(vector, vector.reciprocal(wsc[0:1, b + 3:b + 4],
                                             wsc[0:1, b + 2:b + 3]))

                v_msc = VC.n

                @block.tensor
                def _(tensor):
                    tensor.wait_ge(VC.sem, v_msc)
                    tensor.matmul(ps[4][:, 0:1], ones_r[:], wsc[0:1, b + 2:b + 3],
                                  start=True, stop=True)
                    TC.tagw(tensor, tensor.matmul(ps[4][:, 1:2], ones_r[:],
                                         wsc[0:1, b + 3:b + 4],
                                         start=True, stop=True))

                t_sc2 = TC.n

                @block.vector
                def _(vector):
                    vector.wait_ge(TC.sem, t_sc2)
                    vector.tensor_copy(wsc_col[:, widx:widx + 1], ps[4][:, 0:1])
                    VC.tagw(vector, vector.tensor_copy(m_col[:, widx:widx + 1], ps[4][:, 1:2]))

                v_sccol2 = VC.n
                sg_st = [None] * ntile
                v_tq = [None] * ntile
                for i in range(ntile):
                    @block.sync
                    def _(sync, i=i):
                        if i >= 2:
                            sync.wait_ge(VC.sem, v_tq[i - 2])
                        else:
                            sync.wait_ge(VC.sem, v_red[ntile - 2 + i])
                        LD.tag(sync.dma_start(out=wqbuf[i % 2][:, 0:cols],
                                              in_=wap[128 * i:128 * (i + 1), :]))

                    ldw = LD.n

                    @block.vector
                    def _(vector, i=i):
                        vector.wait_ge(LD.sem, ldw)
                        if i == 0:
                            vector.wait_ge(VC.sem, v_sccol2)
                        if i >= 2:
                            vector.wait_ge(SG.sem, sg_st[i - 2])
                        vector.tensor_scalar(gl[:, 0:cols], wqbuf[i % 2][:, 0:cols],
                                             wsc_col[:, widx:widx + 1], 0.5,
                                             ALU.mult, ALU.is_ge)
                        vsw(vector, vector.tensor_scalar(
                            ll[:, 0:cols], wqbuf[i % 2][:, 0:cols],
                            wsc_col[:, widx:widx + 1], -0.5,
                            ALU.mult, ALU.is_le))
                        v_tq[i] = VC.tagw(vector, vector.tensor_sub(
                            tb[i % 2][:, 0:cols], gl[:, 0:cols], ll[:, 0:cols]))

                    @block.gpsimd
                    def _(gpsimd, i=i):
                        gpsimd.wait_ge(VC.sem, v_tq[i])
                        sg_st[i] = SG.tag(gpsimd.dma_start(
                            out=tdst[128 * i:128 * (i + 1), :],
                            in_=tb[i % 2][:, 0:cols]))

                return [v_tq[ntile - 2], v_tq[ntile - 1]]

            g1 = emit_wq(w1, t1d, DH, DIM, 0)
            emit_wq(w2, t2d, DIM, DH, 1, prev_guard=g1)

            @block.vector
            def _(vector):
                VC.tagw(vector, vector.tensor_single_scalar(
                    m127[:], m_col[:], 1.0 / 127.0, ALU.mult))

            barrier()

        p01.close()

        with ExitStack() as p2s:
            e2 = p2s.enter_context
            mask65 = e2(nc.sbuf_tensor("mask65", [65, 512], F32))
            ga_rep = e2(nc.sbuf_tensor("ga_rep", [128, DIM], F32))
            vsb = e2(nc.sbuf_tensor("vsb", [10, DIM], F32))
            cums_sb = e2(nc.sbuf_tensor("cums_sb", [11, DIM], F32))
            carryv = e2(nc.sbuf_tensor("carryv", [1, DIM], F32))
            woutr = [e2(nc.sbuf_tensor(f"woutr{i}", [128, DIM], F32))
                     for i in range(8)]
            xblk = [e2(nc.sbuf_tensor(f"xblk{i}", [128, DIM], F32)) for i in range(2)]
            n1t = e2(nc.sbuf_tensor("n1t", [128, DIM], F32))
            ascr2 = e2(nc.sbuf_tensor("ascr2", [128, DIM], BF16))
            st2p = [e2(nc.sbuf_tensor(f"st2_{i}", [128, 8], F32))
                    for i in range(2)]
            hT = [e2(nc.sbuf_tensor(f"hT{i}", [128, 512], F32)) for i in range(8)]
            wqb = [e2(nc.sbuf_tensor(f"wqkvb{i}", [128, 512], F32)) for i in range(2)]
            qsl = [e2(nc.sbuf_tensor(f"qsl{i}", [128, 512], F32)) for i in range(8)]
            ksl = [e2(nc.sbuf_tensor(f"ksl{i}", [128, 512], F32)) for i in range(8)]
            vext = [e2(nc.sbuf_tensor(f"vext{i}", [65, DIM], F32)) for i in range(8)]
            Pe2 = [e2(nc.sbuf_tensor(f"Pe{i}", [65, 512], F32))
                   for i in range(2)]
            Po2 = [e2(nc.sbuf_tensor(f"Po{i}", [65, 512], F32))
                   for i in range(2)]
            rcb = [e2(nc.sbuf_tensor(f"rcb{i}", [33, 512], F32))
                   for i in range(2)]
            ao = [e2(nc.sbuf_tensor(f"ao{i}", [128, 512], F32)) for i in range(8)]
            ksc = e2(nc.sbuf_tensor("ksc", [128, 22], F32))
            kx = e2(nc.sbuf_tensor("kx", [128, 11], F32))
            kpadE = e2(nc.sbuf_tensor("kpadE", [128, 64], F32))
            kpadO = e2(nc.sbuf_tensor("kpadO", [128, 64], F32))
            sgm = e2(nc.sbuf_tensor("sgm", [33, 8], F32))

            @block.gpsimd
            def _(gpsimd):
                GC.tag(gpsimd.memset(sgm[:], 0.0))
                GC.tag(gpsimd.memset(mask65[:], 1.0))
                gpsimd.wait_ge(GC.sem, GC.n)
                GC.tagw(gpsimd, gpsimd.affine_select(
                    out=mask65[0:64, :], in_=mask65[0:64, :],
                    compare_op=ALU.is_ge, fill=0.0, base=0,
                    pattern=[[0, 8], [1, 64]], channel_multiplier=-1))

            g_p2c = GC.n

            @block.sync
            def _(sync):
                for i in range(8):
                    LD.tag(sync.dma_start(out=woutr[i][:],
                                          in_=woutd[128 * i:128 * (i + 1), :]))

            ld_wout = LD.n

            @block.tensor
            def _(tensor):
                tensor.matmul(ps[0][:, 0:512], ones_r[:], ga_row[0:1, 0:512],
                              start=True, stop=True)
                TC.tagw(tensor, tensor.matmul(ps[1][:, 0:512], ones_r[:],
                                     ga_row[0:1, 512:1024],
                                     start=True, stop=True))

            t_gar = TC.n

            @block.vector
            def _(vector):
                vector.wait_ge(TC.sem, t_gar)
                vector.tensor_copy(ga_rep[:, 0:512], ps[0][:, 0:512])
                VC.tagw(vector, vector.tensor_copy(ga_rep[:, 512:1024], ps[1][:, 0:512]))

            NBLK = len(ABLK)
            for bi, (off, Tb) in enumerate(ABLK):
                nt = Tb // 128
                nb = Tb // 64
                last_blk = (bi == NBLK - 1)

                t_tr = [None] * nt
                v_hTe = [None] * nt
                v_n1l = [None] * nt
                for i in range(nt):
                    @block.sync
                    def _(sync, i=i):
                        if i >= 2:
                            sync.wait_ge(VC.sem, v_n1l[i - 2])
                        LD.tag(sync.dma_start(
                            out=xblk[i % 2][:],
                            in_=xext[off + 128 * i:off + 128 * (i + 1), :]))

                    ld_i = LD.n
                    ln_stats(xblk[i % 2], st2p[i % 2], ascr2, wait_ld=ld_i)

                    @block.vector
                    def _(vector, i=i):
                        if i >= 2:
                            vector.wait_ge(TC.sem, t_tr[i - 2])
                        v_n1l[i] = VC.tagw(vector, vector.tensor_scalar(
                            n1t[:], xblk[i % 2][:], st2p[i % 2][:, 2:3],
                            st2p[i % 2][:, 7:8], ALU.subtract, ALU.mult))

                    @block.tensor
                    def _(tensor, i=i):
                        tensor.wait_ge(VC.sem, v_n1l[i])
                        if i >= 2:
                            tensor.wait_ge(VC.sem, v_hTe[i - 2])
                        for f in range(8):
                            bank = 2 + (i % 2) * 2 + f // 4
                            mm = tensor.transpose(
                                ps[bank][:, 128 * (f % 4):128 * (f % 4) + 128],
                                n1t[:, 128 * f:128 * (f + 1)], ident[:])
                        t_tr[i] = TC.tagw(tensor, mm)

                    @block.vector
                    def _(vector, i=i):
                        vector.wait_ge(TC.sem, t_tr[i])
                        for f in range(8):
                            bank = 2 + (i % 2) * 2 + f // 4
                            mm = vector.tensor_scalar(
                                hT[f][:, 128 * i:128 * (i + 1)],
                                ps[bank][:, 128 * (f % 4):128 * (f % 4) + 128],
                                ca1_cols[:, f:f + 1], sa_cols[:, f:f + 1],
                                ALU.mult, ALU.add)
                        v_hTe[i] = VC.tagw(vector, mm)

                if bi == 0:
                    @block.vector
                    def _(vector):
                        vector.wait_ge(VC.sem, v_hTe[nt - 1])
                        for f in range(8):
                            mm = vector.tensor_scalar_mul(
                                hT[f][:, 0:128], hT[f][:, 0:128], haslc[:, 0:1])
                        VC.tagw(vector, mm)
                    v_hT_done = VC.n
                else:
                    v_hT_done = v_hTe[nt - 1]

                last_evac = v_hT_done
                v_radd = []
                prev_t_qf = None
                for og in range(6):
                    ld_f = []
                    t_qf = [None] * 8
                    extra = (bi == 0 and og in (2, 3)) or                             (last_blk and og in (0, 1, 4, 5))
                    rtok = (last_blk and og in (0, 1, 4, 5))
                    for f in range(8):
                        @block.sync
                        def _(sync, f=f, og=og, prev_t_qf=prev_t_qf):
                            if f >= 2:
                                sync.wait_ge(TC.sem, t_qf[f - 2])
                            elif prev_t_qf is not None:
                                sync.wait_ge(TC.sem, prev_t_qf[6 + f])
                            ld_f.append(LD.tag(sync.dma_start(
                                out=wqb[f % 2][:],
                                in_=wqkv[128 * f:128 * (f + 1),
                                         512 * og:512 * (og + 1)])))

                        @block.tensor
                        def _(tensor, f=f, og=og, extra=extra, rtok=rtok):
                            tensor.wait_ge(LD.sem, ld_f[f])
                            if f == 0:
                                tensor.wait_ge(VC.sem, last_evac)
                                if bi == 0 and og == 0:
                                    tensor.wait_ge(GC.sem, g_p2c)
                            if extra and len(v_radd) >= 2:
                                tensor.wait_ge(VC.sem, v_radd[-2])
                            if og < 4:
                                for o4 in range(4):
                                    mm = tensor.matmul(
                                        ps[o4][:, 0:Tb],
                                        wqb[f % 2][:, 128 * o4:128 * (o4 + 1)],
                                        hT[f][:, 0:Tb],
                                        start=(f == 0), stop=(f == 7))
                                if bi == 0 and og in (2, 3):
                                    for o4 in range(4):
                                        mm = tensor.matmul(
                                            ps[4 + f % 2][:, o4:o4 + 1],
                                            wqb[f % 2][:, 128 * o4:128 * (o4 + 1)],
                                            hmodT[:, f:f + 1],
                                            start=(o4 == 0), stop=(o4 == 3))
                            else:
                                for tc in range(nt):
                                    mm = tensor.matmul(
                                        ps[tc][:, 0:512],
                                        hT[f][:, 128 * tc:128 * (tc + 1)],
                                        wqb[f % 2][:, 0:512],
                                        start=(f == 0), stop=(f == 7))
                                if bi == 0 and og in (4, 5):
                                    mm = tensor.matmul(
                                        ps[6][0:1, 0:512], hmodT[:, f:f + 1],
                                        wqb[f % 2][:, 0:512],
                                        start=(f == 0), stop=(f == 7))
                            if rtok:
                                for o4 in range(4):
                                    mm = tensor.matmul(
                                        ps[4 + f % 2][:, 4 + o4:5 + o4],
                                        wqb[f % 2][:, 128 * o4:128 * (o4 + 1)],
                                        hr_cols[:, f:f + 1],
                                        start=(o4 == 0), stop=(o4 == 3))
                            t_qf[f] = TC.tagw(tensor, mm)

                        if extra:
                            @block.vector
                            def _(vector, f=f, og=og, rtok=rtok):
                                vector.wait_ge(TC.sem, t_qf[f])
                                if bi == 0 and og in (2, 3):
                                    bofs = (og - 2) * 4
                                    mm = vector.tensor_add(
                                        basek[:, bofs:bofs + 4],
                                        basek[:, bofs:bofs + 4],
                                        ps[4 + f % 2][:, 0:4])
                                if rtok:
                                    dst = qr_sb if og < 2 else vr_sb
                                    bofs = (og % 2) * 4
                                    mm = vector.tensor_add(
                                        dst[:, bofs:bofs + 4],
                                        dst[:, bofs:bofs + 4],
                                        ps[4 + f % 2][:, 4:8])
                                v_radd.append(VC.tagw(vector, mm))

                    @block.vector
                    def _(vector, og=og):
                        vector.wait_ge(TC.sem, t_qf[7])
                        if og < 4:
                            for o4 in range(4):
                                dst = (qsl if og < 2 else ksl)[(og % 2) * 4 + o4]
                                mm = vector.tensor_copy(dst[:, 0:Tb],
                                                        ps[o4][:, 0:Tb])
                        else:
                            vc0 = 512 * (og - 4)
                            for tc in range(nt):
                                vector.tensor_copy(
                                    vext[2 * tc][0:64, vc0:vc0 + 512],
                                    ps[tc][0:64, 0:512])
                                mm = vector.tensor_copy(
                                    vext[2 * tc + 1][0:64, vc0:vc0 + 512],
                                    ps[tc][64:128, 0:512])
                            if bi == 0 and og in (4, 5):
                                mm = vector.tensor_copy(carryv[0:1, vc0:vc0 + 512],
                                                        ps[6][0:1, 0:512])
                        VC.tagw(vector, mm)
                    last_evac = VC.n
                    prev_t_qf = t_qf

                if bi == 0:
                    @block.vector
                    def _(vector):
                        vector.wait_ge(VC.sem, last_evac)
                        VC.tagw(vector, vector.tensor_copy(carryk[:], basek[:]))
                    last_evac = VC.n

                @block.tensor
                def _(tensor, nb=nb):
                    tensor.wait_ge(VC.sem, last_evac)
                    for c in range(nb):
                        for s in range(2):
                            mm = tensor.matmul(ps[s][0:10, 0:512],
                                               onesVs[c][0:64, 0:10],
                                               vext[c][0:64, 512 * s:512 * (s + 1)],
                                               start=(c == 0), stop=(c == nb - 1))
                    TC.tagw(tensor, mm)

                t_vs = TC.n

                @block.vector
                def _(vector):
                    vector.wait_ge(TC.sem, t_vs)
                    vector.tensor_copy(vsb[0:10, 0:512], ps[0][0:10, 0:512])
                    VC.tagw(vector, vector.tensor_copy(vsb[0:10, 512:1024],
                                              ps[1][0:10, 0:512]))

                v_vsb = VC.n

                @block.tensor
                def _(tensor):
                    tensor.wait_ge(VC.sem, v_vsb)
                    tensor.matmul(ps[2][0:11, 0:512], tri[:], vsb[0:10, 0:512],
                                  start=True, stop=False)
                    tensor.matmul(ps[2][0:11, 0:512], ones11[:],
                                  carryv[0:1, 0:512], start=False, stop=True)
                    tensor.matmul(ps[3][0:11, 0:512], tri[:], vsb[0:10, 512:1024],
                                  start=True, stop=False)
                    TC.tagw(tensor, tensor.matmul(ps[3][0:11, 0:512], ones11[:],
                                         carryv[0:1, 512:1024],
                                         start=False, stop=True))

                t_cumv = TC.n

                @block.vector
                def _(vector):
                    vector.wait_ge(TC.sem, t_cumv)
                    vector.tensor_single_scalar(cums_sb[0:11, 0:512],
                                                ps[2][0:11, 0:512],
                                                VSCALE, ALU.mult)
                    vector.tensor_single_scalar(cums_sb[0:11, 512:1024],
                                                ps[3][0:11, 0:512],
                                                VSCALE, ALU.mult)
                    vector.tensor_single_scalar(carryv[0:1, 0:512],
                                                ps[2][0:1, 0:512],
                                                1.0, ALU.mult)
                    VC.tagw(vector, vector.tensor_single_scalar(carryv[0:1, 512:1024],
                                                       ps[3][0:1, 0:512],
                                                       1.0, ALU.mult))

                v_cumsb = VC.n
                sg_vf = []

                @block.gpsimd
                def _(gpsimd, nb=nb):
                    gpsimd.wait_ge(VC.sem, v_cumsb)
                    for c in range(nb):
                        sg_vf.append(SG.tag(gpsimd.dma_start(
                            out=vext[c][64:65, :], in_=cums_sb[c + 1:c + 2, :])))

                v_cumv = VC.n
                sg_vfill = SG.n

                v_ao_last = v_cumv
                prev_a_exp = None
                Wb = 64 * nb
                v_kpad_l = [None] * 8

                @block.vector
                def _(vector, nb=nb):
                    vector.wait_ge(VC.sem, v_cumv)
                    for hp in range(8):
                        kv = ksl[hp][:].rearrange("p (c t) -> p c t", t=64)
                        vector.tensor_reduce(ksc[:, 0:nb, None],
                                             kv[:, 0:nb, :], AX.X, ALU.add)
                        vector.tensor_copy(kx[:, 0:1], carryk[:, hp:hp + 1])
                        vsw(vector, vector.tensor_tensor_scan(
                            ksc[:, 10:10 + nb], ksc[:, 0:nb],
                            ksc[:, 0:nb], carryk[:, hp:hp + 1],
                            ALU.add, ALU.bypass))
                        if nb > 1:
                            vector.tensor_copy(kx[:, 1:nb],
                                               ksc[:, 10:10 + nb - 1])
                        vector.tensor_copy(carryk[:, hp:hp + 1],
                                           ksc[:, 9 + nb:10 + nb])
                        vector.tensor_tensor(
                            kpadE[:, 8 * hp:8 * hp + nb], kx[:, 0:nb],
                            parmask[:, 0:1].broadcast_to([128, nb]), ALU.mult)
                        v_kpad_l[hp] = VC.tagw(vector, vector.tensor_tensor(
                            kpadO[:, 8 * hp:8 * hp + nb], kx[:, 0:nb],
                            parmask[:, 1:2].broadcast_to([128, nb]), ALU.mult))
                t_sc_l = [None] * 8
                a_exp_l = [None] * 8
                v_mask_l = [None] * 8
                t_dn_l = [None] * 8
                v_rc_l = [None] * 8
                t_rp_l = [None] * 8
                v_norm_l = [None] * 8
                t_ao_l = [None] * 8
                v_evac_l = [None] * 8

                for step in range(10):
                    hp = step
                    pm = step - 1
                    p2_ = step - 2

                    if hp < 8:
                        sb_ = hp % 2

                        @block.tensor
                        def _(tensor, hp=hp, nb=nb, sb_=sb_,
                              v_ao_last=v_ao_last, prev_a_exp=prev_a_exp):
                            tensor.wait_ge(VC.sem, v_kpad_l[hp])
                            if prev_a_exp is None:
                                tensor.wait_ge(VC.sem, v_ao_last)
                            else:
                                tensor.wait_ge(AC.sem, prev_a_exp)
                            tensor.wait_ge(SG.sem, sg_vfill)
                            cd = ps[2 if hp % 2 == 0 else 7]
                            for c in range(nb):
                                cc = slice(64 * c, 64 * (c + 1))
                                tensor.matmul(ps[sb_][0:64, cc],
                                              ksl[hp][0:64, cc], qsl[hp][0:64, cc],
                                              start=(c == 0), stop=(c == nb - 1),
                                              tile_position=(0, 0))
                            for c in range(nb):
                                cc = slice(64 * c, 64 * (c + 1))
                                tensor.matmul(ps[sb_][64:128, cc],
                                              ksl[hp][64:128, cc],
                                              qsl[hp][64:128, cc],
                                              start=(c == 0), stop=(c == nb - 1),
                                              tile_position=(64, 64))
                            for c in range(nb):
                                cc = slice(64 * c, 64 * (c + 1))
                                tensor.matmul(cd[0:1, cc],
                                              kpadE[:, 8 * hp + c:8 * hp + c + 1],
                                              qsl[hp][:, cc],
                                              start=(c == 0), stop=(c == nb - 1),
                                              tile_position=(0, 0))
                                mm = tensor.matmul(cd[32:33, cc],
                                                   kpadO[:, 8 * hp + c:
                                                          8 * hp + c + 1],
                                                   qsl[hp][:, cc],
                                                   start=(c == 0),
                                                   stop=(c == nb - 1),
                                                   tile_position=(0, 32))
                            t_sc_l[hp] = TC.tagw(tensor, mm)

                    if pm >= 0 and pm < 8:
                        @block.scalar
                        def _(scalar, pm=pm, Wb=Wb):
                            scalar.wait_ge(TC.sem, t_sc_l[pm])
                            if pm >= 2:
                                scalar.wait_ge(TC.sem, t_ao_l[pm - 2])
                            Pe, Po = Pe2[pm % 2], Po2[pm % 2]
                            sbp = ps[pm % 2]
                            cd = ps[2 if pm % 2 == 0 else 7]
                            scalar.activation(Pe[0:64, 0:Wb], sbp[0:64, 0:Wb],
                                              AF.Exp, bias=-SOFTB, scale=SCALE)
                            scalar.activation(Po[0:64, 0:Wb],
                                              sbp[64:128, 0:Wb],
                                              AF.Exp, bias=-SOFTB, scale=SCALE)
                            scalar.activation(Pe[64:65, 0:Wb], cd[0:1, 0:Wb],
                                              AF.Sigmoid, scale=CSCALE)
                            a_exp_l[pm] = AC.tagw(scalar, scalar.activation(
                                Po[64:65, 0:Wb], cd[32:33, 0:Wb],
                                AF.Sigmoid, scale=CSCALE))

                        prev_a_exp = a_exp_l[pm]

                        @block.vector
                        def _(vector, pm=pm, Wb=Wb):
                            vector.wait_ge(AC.sem, a_exp_l[pm])
                            Pe, Po = Pe2[pm % 2], Po2[pm % 2]
                            vector.tensor_mul(Pe[0:64, 0:Wb], Pe[0:64, 0:Wb],
                                              mask65[0:64, 0:Wb])
                            v_mask_l[pm] = VC.tagw(vector, vector.tensor_mul(
                                Po[0:64, 0:Wb], Po[0:64, 0:Wb],
                                mask65[0:64, 0:Wb]))

                        @block.tensor
                        def _(tensor, pm=pm, Wb=Wb):
                            tensor.wait_ge(VC.sem, v_mask_l[pm])
                            Pe, Po = Pe2[pm % 2], Po2[pm % 2]
                            tensor.matmul(ps[3][0:33, 0:Wb], onesE[:],
                                          Pe[0:64, 0:Wb], start=True, stop=False)
                            t_dn_l[pm] = TC.tagw(tensor, tensor.matmul(
                                ps[3][0:33, 0:Wb], onesO[:], Po[0:64, 0:Wb],
                                start=False, stop=True))

                    if p2_ >= 0 and p2_ < 8:
                        @block.tensor
                        def _(tensor, p2_=p2_, Wb=Wb):
                            tensor.wait_ge(VC.sem, v_rc_l[p2_])
                            rc = rcb[p2_ % 2]
                            tensor.matmul(ps[4][0:64, 0:Wb], ones_r[0:1, 0:64],
                                          rc[0:1, 0:Wb], start=True, stop=True,
                                          tile_position=(0, 0))
                            t_rp_l[p2_] = TC.tagw(tensor, tensor.matmul(
                                ps[4][64:128, 0:Wb], halfsel[32:33, 64:128],
                                rc[32:33, 0:Wb], start=True, stop=True,
                                tile_position=(32, 64)))

                        @block.vector
                        def _(vector, p2_=p2_, Wb=Wb):
                            vector.wait_ge(TC.sem, t_rp_l[p2_])
                            Pe, Po = Pe2[p2_ % 2], Po2[p2_ % 2]
                            vector.tensor_mul(Pe[0:64, 0:Wb], Pe[0:64, 0:Wb],
                                              ps[4][0:64, 0:Wb])
                            v_norm_l[p2_] = VC.tagw(vector, vector.tensor_mul(
                                Po[0:64, 0:Wb], Po[0:64, 0:Wb],
                                ps[4][64:128, 0:Wb]))

                        @block.tensor
                        def _(tensor, p2_=p2_, nb=nb):
                            tensor.wait_ge(VC.sem, v_norm_l[p2_])
                            ab_ = 5 + p2_ % 2
                            Pe, Po = Pe2[p2_ % 2], Po2[p2_ % 2]
                            for c in range(nb):
                                cc = slice(64 * c, 64 * (c + 1))
                                tensor.matmul(ps[ab_][0:64, cc],
                                              vext[c][0:65,
                                                      128 * p2_:128 * p2_ + 64],
                                              Pe[0:65, cc],
                                              start=(c == 0), stop=(c == nb - 1),
                                              tile_position=(0, 0))
                            for c in range(nb):
                                cc = slice(64 * c, 64 * (c + 1))
                                mm = tensor.matmul(
                                    ps[ab_][64:128, cc],
                                    vext[c][0:65,
                                            128 * p2_ + 64:128 * (p2_ + 1)],
                                    Po[0:65, cc],
                                    start=(c == 0), stop=(c == nb - 1),
                                    tile_position=(0, 64))
                            t_ao_l[p2_] = TC.tagw(tensor, mm)

                    if pm >= 0 and pm < 8:
                        @block.vector
                        def _(vector, pm=pm, Wb=Wb):
                            vector.wait_ge(TC.sem, t_dn_l[pm])
                            rc = rcb[pm % 2]
                            vector.reciprocal(rc[0:1, 0:Wb], ps[3][0:1, 0:Wb])
                            v_rc_l[pm] = VC.tagw(vector, vector.reciprocal(
                                rc[32:33, 0:Wb], ps[3][32:33, 0:Wb]))

                    if p2_ >= 0 and p2_ < 8:
                        @block.vector
                        def _(vector, p2_=p2_, Wb=Wb):
                            vector.wait_ge(TC.sem, t_ao_l[p2_])
                            ab_ = 5 + p2_ % 2
                            v_evac_l[p2_] = VC.tagw(vector, vector.tensor_copy(
                                ao[p2_][:, 0:Wb], ps[ab_][:, 0:Wb]))

                v_ao_last = v_evac_l[7]

                v_x1 = [None] * nt
                sg_x1 = [None] * nt
                for tc in range(nt):
                    @block.sync
                    def _(sync, tc=tc):
                        if tc >= 2:
                            sync.wait_ge(SG.sem, sg_x1[tc - 2])
                        else:
                            sync.wait_ge(VC.sem, v_ao_last)
                        LD.tag(sync.dma_start(
                            out=xblk[tc % 2][:],
                            in_=xext[off + 128 * tc:off + 128 * (tc + 1), :]))

                    ld_res = LD.n

                    @block.tensor
                    def _(tensor, tc=tc, v_ao_last=v_ao_last):
                        tensor.wait_ge(VC.sem, v_ao_last)
                        if bi == 0 and tc == 0:
                            tensor.wait_ge(LD.sem, ld_wout)
                        if tc >= 2:
                            tensor.wait_ge(VC.sem, v_x1[tc - 2])
                        pb = (tc % 2) * 2
                        for hp in range(8):
                            tensor.matmul(ps[pb][:, 0:512],
                                          ao[hp][:, 128 * tc:128 * (tc + 1)],
                                          woutr[hp][:, 0:512],
                                          start=(hp == 0), stop=(hp == 7))
                            mm = tensor.matmul(ps[pb + 1][:, 0:512],
                                               ao[hp][:, 128 * tc:128 * (tc + 1)],
                                               woutr[hp][:, 512:1024],
                                               start=(hp == 0), stop=(hp == 7))
                        TC.tagw(tensor, mm)

                    t_y = TC.n

                    @block.vector
                    def _(vector, tc=tc, t_y=t_y, ld_res=ld_res):
                        vector.wait_ge(TC.sem, t_y)
                        vector.wait_ge(LD.sem, ld_res)
                        pb = (tc % 2) * 2
                        vector.tensor_mul(n1t[:, 0:512], ps[pb][:, 0:512],
                                          ga_rep[:, 0:512])
                        vector.tensor_mul(n1t[:, 512:1024], ps[pb + 1][:, 0:512],
                                          ga_rep[:, 512:1024])
                        vector.tensor_add(xblk[tc % 2][:, 0:512],
                                          xblk[tc % 2][:, 0:512], n1t[:, 0:512])
                        v_x1[tc] = VC.tagw(vector, vector.tensor_add(
                            xblk[tc % 2][:, 512:1024],
                            xblk[tc % 2][:, 512:1024], n1t[:, 512:1024]))

                    @block.gpsimd
                    def _(gpsimd, tc=tc):
                        gpsimd.wait_ge(VC.sem, v_x1[tc])
                        sg_x1[tc] = SG.tag(gpsimd.dma_start(
                            out=x1d[off + 128 * tc:off + 128 * (tc + 1), :],
                            in_=xblk[tc % 2][:]))

                barrier()

            @block.vector
            def _(vector):
                vector.tensor_tensor(kpadE[:, 0:8], carryk[:],
                                     parmask[:, 0:1].broadcast_to([128, 8]),
                                     ALU.mult)
                VC.tagw(vector, vector.tensor_tensor(
                    kpadO[:, 0:8], carryk[:],
                    parmask[:, 1:2].broadcast_to([128, 8]), ALU.mult))

            v_cpad = VC.n

            @block.tensor
            def _(tensor):
                tensor.wait_ge(VC.sem, v_cpad)
                for hp in range(8):
                    tensor.matmul(ps[0][0:1, hp:hp + 1], kpadE[:, hp:hp + 1],
                                  qr_sb[:, hp:hp + 1],
                                  start=(hp == 0), stop=(hp == 7),
                                  tile_position=(0, 0))
                    tensor.matmul(ps[0][32:33, hp:hp + 1], kpadO[:, hp:hp + 1],
                                  qr_sb[:, hp:hp + 1],
                                  start=(hp == 0), stop=(hp == 7),
                                  tile_position=(0, 32))
                for f in range(8):
                    mm = tensor.transpose(ps[1][:, f:f + 1],
                                          carryv[0:1, 128 * f:128 * (f + 1)],
                                          ident1[:])
                TC.tagw(tensor, mm)

            t_rcd = TC.n

            @block.scalar
            def _(scalar):
                scalar.wait_ge(TC.sem, t_rcd)
                scalar.activation(sgm[0:1, :], ps[0][0:1, 0:8],
                                  AF.Sigmoid, scale=CSCALE)
                AC.tagw(scalar, scalar.activation(sgm[32:33, :], ps[0][32:33, 0:8],
                                         AF.Sigmoid, scale=CSCALE))

            a_rsg = AC.n

            @block.tensor
            def _(tensor):
                tensor.wait_ge(AC.sem, a_rsg)
                TC.tagw(tensor, tensor.matmul(ps[2][:, 0:8], halfsel[:], sgm[:],
                                     start=True, stop=True))

            t_rsg = TC.n

            @block.vector
            def _(vector):
                vector.wait_ge(TC.sem, t_rsg)
                vsw(vector, vector.tensor_copy(ksc[:, 0:8], ps[1][:, 0:8]))
                vsw(vector, vector.tensor_mul(ksc[:, 0:8], ksc[:, 0:8],
                                              ps[2][:, 0:8]))
                vsw(vector, vector.tensor_scalar_mul(ksc[:, 0:8], ksc[:, 0:8],
                                                     VSCALE))
                VC.tagw(vector, vector.tensor_add(ksc[:, 0:8], ksc[:, 0:8], vr_sb[:]))

            v_rout = VC.n

            @block.tensor
            def _(tensor):
                tensor.wait_ge(VC.sem, v_rout)
                for hp in range(8):
                    tensor.matmul(ps[3][0:1, 0:512], ksc[:, hp:hp + 1],
                                  woutr[hp][:, 0:512],
                                  start=(hp == 0), stop=(hp == 7))
                    mm = tensor.matmul(ps[4][0:1, 0:512], ksc[:, hp:hp + 1],
                                       woutr[hp][:, 512:1024],
                                       start=(hp == 0), stop=(hp == 7))
                TC.tagw(tensor, mm)

            t_ry = TC.n

            @block.vector
            def _(vector):
                vector.wait_ge(TC.sem, t_ry)
                vector.tensor_mul(tmp_row[0:1, 0:512], ps[3][0:1, 0:512],
                                  ga_row[0:1, 0:512])
                vector.tensor_mul(tmp_row[0:1, 512:1024], ps[4][0:1, 0:512],
                                  ga_row[0:1, 512:1024])
                vector.tensor_add(xr_row[0:1, 0:512], xr_row[0:1, 0:512],
                                  tmp_row[0:1, 0:512])
                vector.tensor_add(xr_row[0:1, 512:1024], xr_row[0:1, 512:1024],
                                  tmp_row[0:1, 512:1024])
                VC.tagw(vector, vector.memset(n1t[:], 0.0))

            v_rx1 = VC.n

            @block.gpsimd
            def _(gpsimd):
                gpsimd.wait_ge(VC.sem, v_rx1)
                SG.tag(gpsimd.dma_start(out=x1d[2176:2177, :], in_=xr_row[:]))
                SG.tag(gpsimd.dma_start(out=x1d[2177:2304, :], in_=n1t[0:127, :]))

            barrier()

        with ExitStack() as p3s:
            e3 = p3s.enter_context
            t1T = [e3(nc.sbuf_tensor(f"t1T{f}", [128, DH], BF16)) for f in range(8)]
            t2T = [e3(nc.sbuf_tensor(f"t2T{h}", [128, DIM], BF16)) for h in range(32)]
            x1b = [e3(nc.sbuf_tensor(f"x1b{i}", [128, DIM], F32)) for i in range(2)]
            st3p = [e3(nc.sbuf_tensor(f"st3_{i}", [128, 8], F32))
                    for i in range(2)]
            st3 = st3p[0]


            ld_t = []

            @block.sync
            def _(sync):
                for f in range(8):
                    ld_t.append(LD.tag(sync.dma_start_transpose(
                        t1T[f][:], t1d[:, 128 * f:128 * (f + 1)])))
                for h in range(32):
                    ld_t.append(LD.tag(sync.dma_start_transpose(
                        t2T[h][:], t2d[:, 128 * h:128 * (h + 1)])))

            ld_tw = LD.n

            with ExitStack() as pks:
                ek = pks.enter_context
                n2t = ek(nc.sbuf_tensor("n2t", [128, DIM], F32))
                h2t = ek(nc.sbuf_tensor("h2t", [128, DIM], F32))
                kqi = ek(nc.sbuf_tensor("kqi", [128, DIM], I32))
                kqf = ek(nc.sbuf_tensor("kqf", [128, DIM], F32))
                cqt = ek(nc.sbuf_tensor("cqt", [128, DIM], F32))
                kqb = [ek(nc.sbuf_tensor(f"kqb{i}", [128, DIM], F32))
                       for i in range(2)]
                kqstg = [ek(nc.sbuf_tensor(f"kqstg{i}", [128, DIM], BF16))
                         for i in range(2)]
                m1stg = ek(nc.sbuf_tensor("m1stg", [1, 512], F32))
                ascr3 = ek(nc.sbuf_tensor("ascr3", [128, DIM], BF16))
                qs3 = ek(nc.sbuf_tensor("qs3", [128, 2], F32))
                scl = ek(nc.sbuf_tensor("scl", [128, 18], F32))
                m1sc = ek(nc.sbuf_tensor("m1sc", [128, 18], F32))

                kqdv = kqd.rearrange("(f p) t -> p f t", f=8)
                NXT = X1R // 128
                v_kq = [None] * NXT
                t_kqT = [None] * NXT
                v_kqTe = [None] * NXT
                sg_kq = [None] * NXT
                for i in range(NXT):
                    @block.sync
                    def _(sync, i=i):
                        if i >= 2:
                            sync.wait_ge(VC.sem, v_kq[i - 2])
                        LD.tag(sync.dma_start(out=x1b[i % 2][:],
                                              in_=x1d[128 * i:128 * (i + 1), :]))

                    ld_i = LD.n
                    ln_stats(x1b[i % 2], st3p[i % 2], ascr3, wait_ld=ld_i)

                    @block.vector
                    def _(vector, i=i):
                        vsw(vector, vector.tensor_scalar(
                            n2t[:], x1b[i % 2][:], st3p[i % 2][:, 2:3],
                            st3p[i % 2][:, 7:8], ALU.subtract, ALU.mult))
                        vsw(vector, vector.tensor_mul(h2t[:], n2t[:],
                                                      cb1_rep[:]))
                        vsw(vector, vector.tensor_add(h2t[:], h2t[:], sb_rep[:]))
                        vsw(vector, vector.tensor_reduce(
                            qs3[:, 0:1], h2t[:], AX.X, ALU.max,
                            apply_absolute_value=True))
                        vector.tensor_scalar(qs3[:, 1:2], qs3[:, 0:1], 1e-5,
                                             1.0 / 127.0, ALU.max, ALU.mult)
                        vector.tensor_scalar(m1sc[:, i:i + 1], qs3[:, 0:1], 1e-5,
                                             m127[:, 0:1], ALU.max, ALU.mult)
                        vsw(vector, vector.reciprocal(scl[:, i:i + 1],
                                                      qs3[:, 1:2]))
                        vsw(vector, vector.tensor_scalar_mul(n2t[:], h2t[:],
                                                             scl[:, i:i + 1]))
                        vsw(vector, vector.tensor_copy(kqi[:], n2t[:]))
                        vsw(vector, vector.tensor_copy(kqf[:], kqi[:]))
                        vsw(vector, vector.tensor_sub(cqt[:], n2t[:], kqf[:]))
                        vector.tensor_single_scalar(h2t[:], cqt[:], -0.5, ALU.is_le)
                        vsw(vector, vector.scalar_tensor_tensor(
                            cqt[:], cqt[:], 0.5, kqf[:], ALU.is_ge, ALU.add))
                        if i >= 2:
                            vector.wait_ge(TC.sem, t_kqT[i - 2])
                        v_kq[i] = VC.tagw(vector, vector.tensor_sub(
                            kqb[i % 2][:], cqt[:], h2t[:]))

                    @block.tensor
                    def _(tensor, i=i):
                        tensor.wait_ge(VC.sem, v_kq[i])
                        if i >= 2:
                            tensor.wait_ge(VC.sem, v_kqTe[i - 2])
                        for f in range(8):
                            bank = (i % 2) * 2 + f // 4
                            mm = tensor.transpose(
                                ps[bank][:, 128 * (f % 4):128 * (f % 4) + 128],
                                kqb[i % 2][:, 128 * f:128 * (f + 1)], ident[:])
                        t_kqT[i] = TC.tagw(tensor, mm)

                    @block.vector
                    def _(vector, i=i):
                        vector.wait_ge(TC.sem, t_kqT[i])
                        if i >= 2:
                            vector.wait_ge(SG.sem, sg_kq[i - 2])
                        haloL = (i == 0)
                        haloR = (i == 17)
                        for f in range(8):
                            bank = (i % 2) * 2 + f // 4
                            mm = vector.tensor_copy(
                                kqstg[i % 2][:, 128 * f:128 * (f + 1)],
                                ps[bank][:, 128 * (f % 4):128 * (f % 4) + 128])
                        kv = kqstg[i % 2][:].rearrange("p (f t) -> p f t", t=128)
                        if haloL or haloR:
                            vsw(vector, mm)
                        if haloL:
                            mm = vector.tensor_scalar_mul(kv[:, :, 127:128],
                                                          kv[:, :, 127:128],
                                                          haslc[:, 0:1])
                        if haloR:
                            mm = vector.tensor_scalar_mul(kv[:, :, 0:1],
                                                          kv[:, :, 0:1],
                                                          hasrc[:, 0:1])
                        v_kqTe[i] = VC.tagw(vector, mm)

                    @block.gpsimd
                    def _(gpsimd, i=i):
                        gpsimd.wait_ge(VC.sem, v_kqTe[i])
                        sg_kq[i] = SG.tag(gpsimd.dma_start(
                            out=kqdv[:, :, 128 * i:128 * (i + 1)],
                            in_=kqstg[i % 2][:].rearrange("p (f t) -> p f t", f=8)))

                v_flagz = v_kqTe[NXT - 1]
                t_m1g = []
                v_m1e = []
                sg_m1 = []
                for grp in range(5):
                    i0, i1 = 4 * grp, min(4 * grp + 4, NXT)

                    @block.tensor
                    def _(tensor, grp=grp, i0=i0, i1=i1):
                        if grp == 0:
                            tensor.wait_ge(VC.sem, v_flagz)
                        else:
                            tensor.wait_ge(VC.sem, v_m1e[grp - 1])
                        for i in range(i0, i1):
                            mm = tensor.transpose(ps[6][0:1, 128 * (i - i0):
                                                        128 * (i - i0) + 128],
                                                  m1sc[:, i:i + 1], ident[:])
                        t_m1g.append(TC.tagw(tensor, mm))

                    @block.vector
                    def _(vector, grp=grp, i0=i0, i1=i1):
                        vector.wait_ge(TC.sem, t_m1g[grp])
                        if grp >= 1:
                            vector.wait_ge(SG.sem, sg_m1[grp - 1])
                        VC.tagw(vector, vector.tensor_copy(
                            m1stg[0:1, 0:128 * (i1 - i0)],
                            ps[6][0:1, 0:128 * (i1 - i0)]))
                        v_m1e.append(VC.n)

                    @block.gpsimd
                    def _(gpsimd, grp=grp, i0=i0, i1=i1):
                        gpsimd.wait_ge(VC.sem, v_m1e[grp])
                        sg_m1.append(SG.tag(gpsimd.dma_start(
                            out=m1d[0:1, 128 * i0:128 * i1],
                            in_=m1stg[0:1, 0:128 * (i1 - i0)])))

                sg_m1row = SG.n

                barrier()

            with ExitStack() as pfs:
                ef = pfs.enter_context
                kw = [ef(nc.sbuf_tensor(f"kw{i}", [128, 8 * 258], BF16))
                      for i in range(2)]
                ybr = [ef(nc.sbuf_tensor(f"ybr{i}", [128, FBLK], F32))
                       for i in range(4)]
                ybq = [ef(nc.sbuf_tensor(f"ybq{i}", [128, FBLK], F32))
                       for i in range(4)]
                k2b = [ef(nc.sbuf_tensor(f"k2b{i}", [128, FBLK], BF16))
                       for i in range(4)]
                repm = ef(nc.sbuf_tensor("repm", [128, FBLK + 2], F32))
                h1s = [ef(nc.sbuf_tensor(f"h1s{i}", [128, FBLK + 2], F32))
                       for i in range(2)]
                cv0 = [ef(nc.sbuf_tensor(f"cv0_{i}", [128, FBLK], F32))
                       for i in range(4)]
                cv1 = [ef(nc.sbuf_tensor(f"cv1_{i}", [128, FBLK], F32))
                       for i in range(2)]
                ymax2 = [ef(nc.sbuf_tensor(f"ymax{i}", [128, FBLK], F32))
                         for i in range(2)]
                sc2r = ef(nc.sbuf_tensor("sc2r", [1, FBLK], F32))
                xs2 = [ef(nc.sbuf_tensor(f"xs2_{i}", [128, FBLK], F32))
                       for i in range(2)]
                k2i = [ef(nc.sbuf_tensor(f"k2i_{i}", [128, FBLK], I32))
                       for i in range(2)]
                k2f = [ef(nc.sbuf_tensor(f"k2f_{i}", [128, FBLK], F32))
                       for i in range(2)]
                n2b = ef(nc.sbuf_tensor("n2b", [128, DIM], F32))
                m1fb = [ef(nc.sbuf_tensor(f"m1fb{i}", [1, FBLK + 2], F32))
                        for i in range(2)]
                yabs = xs2
                c2t = cv0
                m2t = cv1

                ld_kw = [None] * (NFB + 1)
                LD_M1 = [None] * NFB
                sg_x2 = [None] * (2 * NFB)
                t_h1_last = [None] * NFB
                NPH = 16
                for fb in range(NFB):
                    w0 = 127 + FBLK * fb
                    r0 = 128 + FBLK * fb

                    @block.sync
                    def _(sync, fb=fb, w0=w0):
                        if fb == 0:
                            sync.wait_ge(SG.sem, sg_kq[NXT - 1])
                            sync.wait_ge(SG.sem, sg_m1row)
                        if fb >= 2:
                            sync.wait_ge(TC.sem, t_h1_last[fb - 2])
                        LD.tag(sync.dma_start(out=m1fb[fb % 2][:],
                                              in_=m1d[0:1, w0:w0 + FBLK + 2]))
                        for f in range(8):
                            LD.tag(sync.dma_start(
                                out=kw[fb % 2][:, 258 * f:258 * (f + 1)],
                                in_=kqd[128 * f:128 * (f + 1), w0:w0 + FBLK + 2]))

                    ld_kwf = LD.n
                    ld_kw[fb] = ld_kwf
                    LD_M1[fb] = ld_kwf

                    @block.tensor
                    def _(tensor, fb=fb, ld_kwf=None):
                        if fb == 0:
                            tensor.wait_ge(LD.sem, ld_tw)
                        tensor.wait_ge(LD.sem, LD_M1[fb])
                        TC.tagw(tensor, tensor.matmul(ps[3][:, 0:FBLK + 2], ones_r[:],
                                             m1fb[fb % 2][:],
                                             start=True, stop=True))

                    t_rep = TC.n

                    @block.vector
                    def _(vector, t_rep=t_rep):
                        vector.wait_ge(TC.sem, t_rep)
                        VC.tagw(vector, vector.tensor_copy(repm[:], ps[3][:, 0:FBLK + 2]))

                    v_repm = VC.n

                    a_yp = [None] * NPH
                    t_h1p = [None] * NPH
                    v_h1p = [None] * NPH
                    v_ymp = [None] * NPH
                    sg_yp = [None] * NPH
                    for ph in range(NPH):
                        h0, h1_ = 2 * ph, 2 * ph + 1

                        @block.tensor
                        def _(tensor, ph=ph, h0=h0, h1_=h1_, fb=fb,
                              v_repm=v_repm, ld_kwf=ld_kwf):
                            if ph < 3:
                                tensor.wait_ge(VC.sem, v_repm)
                            if ph == 0:
                                tensor.wait_ge(LD.sem, ld_kwf)
                            if ph >= 3:
                                tensor.wait_ge(VC.sem, v_h1p[ph - 3])
                            for hc in (h0, h1_):
                                for k in range(8):
                                    mm = tensor.matmul(
                                        ps[hc % 6][:, 0:FBLK + 2],
                                        t1T[k][:, 128 * hc:128 * (hc + 1)],
                                        kw[fb % 2][:, 258 * k:258 * (k + 1)],
                                        start=(k == 0), stop=(k == 7))
                            t_h1p[ph] = TC.tagw(tensor, mm)
                            if ph == NPH - 1:
                                t_h1_last[fb] = t_h1p[ph]

                        @block.vector
                        def _(vector, ph=ph, h0=h0, h1_=h1_):
                            vector.wait_ge(TC.sem, t_h1p[ph])
                            if ph >= 2:
                                vector.wait_ge(AC.sem, a_yp[ph - 2])
                            cA, cB = cv0[h0 % 4], cv0[h1_ % 4]
                            vector.tensor_mul(h1s[0][:], ps[h0 % 6][:, 0:FBLK + 2],
                                              repm[:])
                            vector.tensor_mul(h1s[1][:], ps[h1_ % 6][:, 0:FBLK + 2],
                                              repm[:])
                            vector.tensor_scalar_mul(cA[:], h1s[0][:, 0:FBLK],
                                                     cw_sb[:, 3 * h0:3 * h0 + 1])
                            vector.tensor_scalar_mul(cB[:], h1s[1][:, 0:FBLK],
                                                     cw_sb[:, 3 * h1_:3 * h1_ + 1])
                            vector.scalar_tensor_tensor(
                                cv1[0][:], h1s[0][:, 1:FBLK + 1],
                                cw_sb[:, 3 * h0 + 1:3 * h0 + 2],
                                cA[:], ALU.mult, ALU.add)
                            vector.scalar_tensor_tensor(
                                cv1[1][:], h1s[1][:, 1:FBLK + 1],
                                cw_sb[:, 3 * h1_ + 1:3 * h1_ + 2],
                                cB[:], ALU.mult, ALU.add)
                            vector.scalar_tensor_tensor(
                                cA[:], h1s[0][:, 2:FBLK + 2],
                                cw_sb[:, 3 * h0 + 2:3 * h0 + 3],
                                cv1[0][:], ALU.mult, ALU.add)
                            v_h1p[ph] = VC.tagw(vector, vector.scalar_tensor_tensor(
                                cB[:], h1s[1][:, 2:FBLK + 2],
                                cw_sb[:, 3 * h1_ + 2:3 * h1_ + 3],
                                cv1[1][:], ALU.mult, ALU.add))

                        @block.scalar
                        def _(scalar, ph=ph, h0=h0, h1_=h1_):
                            scalar.wait_ge(VC.sem, v_h1p[ph])
                            if ph >= 2:
                                scalar.wait_ge(SG.sem, sg_yp[ph - 2])
                            if ph >= 1:
                                scalar.wait_ge(VC.sem, v_ymp[ph - 1])
                            scalar.activation(ybr[h0 % 4][:], cv0[h0 % 4][:],
                                              AF.Silu, bias=cb_sb[:, h0:h0 + 1])
                            scalar.activation(ybr[h1_ % 4][:], cv0[h1_ % 4][:],
                                              AF.Silu, bias=cb_sb[:, h1_:h1_ + 1])
                            scalar.activation(yabs[0][:], ybr[h0 % 4][:], AF.Abs)
                            a_yp[ph] = AC.tagw(scalar, scalar.activation(
                                yabs[1][:], ybr[h1_ % 4][:], AF.Abs))

                        @block.vector
                        def _(vector, ph=ph):
                            vector.wait_ge(AC.sem, a_yp[ph])
                            if ph == 0:
                                vector.tensor_copy(ymax2[0][:], yabs[0][:])
                                mm = vector.tensor_copy(ymax2[1][:], yabs[1][:])
                            else:
                                vector.tensor_max(ymax2[0][:], ymax2[0][:],
                                                  yabs[0][:])
                                mm = vector.tensor_max(ymax2[1][:], ymax2[1][:],
                                                       yabs[1][:])
                            v_ymp[ph] = VC.tagw(vector, mm)

                        @block.gpsimd
                        def _(gpsimd, ph=ph, h0=h0, h1_=h1_, fb=fb):
                            gpsimd.wait_ge(VC.sem, v_ymp[ph])
                            SG.tag(gpsimd.dma_start(
                                out=yd[32 * fb + h0:32 * fb + h0 + 1, :]
                                    .rearrange("o (p c) -> (o p) c", p=128),
                                in_=ybr[h0 % 4][:]))
                            sg_yp[ph] = SG.tag(gpsimd.dma_start(
                                out=yd[32 * fb + h1_:32 * fb + h1_ + 1, :]
                                    .rearrange("o (p c) -> (o p) c", p=128),
                                in_=ybr[h1_ % 4][:]))

                    @block.tensor
                    def _(tensor):
                        tensor.wait_ge(VC.sem, v_ymp[NPH - 1])
                        tensor.transpose(ps[0][:, 0:128], ymax2[0][:, 0:128],
                                         ident[:])
                        tensor.transpose(ps[0][:, 128:256], ymax2[0][:, 128:256],
                                         ident[:])
                        tensor.transpose(ps[0][:, 256:384], ymax2[1][:, 0:128],
                                         ident[:])
                        TC.tagw(tensor, tensor.transpose(
                            ps[0][:, 384:512], ymax2[1][:, 128:256], ident[:]))

                    t_ymT = TC.n

                    @block.vector
                    def _(vector, t_ymT=t_ymT):
                        vector.wait_ge(TC.sem, t_ymT)
                        vector.tensor_reduce(st3[:, 0:1], ps[0][:, 0:128],
                                             AX.X, ALU.max)
                        vector.tensor_reduce(st3[:, 4:5], ps[0][:, 256:384],
                                             AX.X, ALU.max)
                        vector.tensor_reduce(st3[:, 1:2], ps[0][:, 128:256],
                                             AX.X, ALU.max)
                        vector.tensor_reduce(st3[:, 5:6], ps[0][:, 384:512],
                                             AX.X, ALU.max)
                        vector.tensor_max(st3[:, 0:1], st3[:, 0:1], st3[:, 4:5])
                        vsw(vector, vector.tensor_max(st3[:, 1:2], st3[:, 1:2],
                                                      st3[:, 5:6]))
                        vector.tensor_scalar(st3[:, 2:4], st3[:, 0:2], 1e-5,
                                             1.0 / 127.0, ALU.max, ALU.mult)
                        vsw(vector, vector.tensor_scalar(
                            st3[:, 0:2], st3[:, 0:2], 1e-5, m127[:, 1:2],
                            ALU.max, ALU.mult))
                        vector.reciprocal(st3[:, 6:7], st3[:, 2:3])
                        VC.tagw(vector, vector.reciprocal(st3[:, 7:8],
                                                          st3[:, 3:4]))

                    v_sc2 = VC.n

                    @block.tensor
                    def _(tensor, v_sc2=v_sc2):
                        tensor.wait_ge(VC.sem, v_sc2)
                        tensor.transpose(ps[1][0:1, 0:128], st3[:, 6:7], ident[:])
                        TC.tagw(tensor, tensor.transpose(ps[1][0:1, 128:256],
                                                st3[:, 7:8], ident[:]))

                    t_s2T = TC.n

                    @block.vector
                    def _(vector, t_s2T=t_s2T):
                        vector.wait_ge(TC.sem, t_s2T)
                        VC.tagw(vector, vector.tensor_copy(sc2r[:], ps[1][0:1, 0:FBLK]))

                    v_s2r = VC.n

                    @block.tensor
                    def _(tensor, v_s2r=v_s2r):
                        tensor.wait_ge(VC.sem, v_s2r)
                        TC.tagw(tensor, tensor.matmul(ps[3][:, 0:FBLK], ones_r[:], sc2r[:],
                                             start=True, stop=True))

                    t_rp2 = TC.n

                    v_k2p = [None] * NPH
                    t_mm2p = [None] * NPH
                    ld_yqp = [None] * NPH
                    for ph in range(NPH):
                        h0, h1_ = 2 * ph, 2 * ph + 1

                        @block.sync
                        def _(sync, ph=ph, h0=h0, h1_=h1_, fb=fb):
                            sync.wait_ge(SG.sem, sg_yp[ph])
                            if ph >= 2:
                                sync.wait_ge(VC.sem, v_k2p[ph - 2])
                            LD.tag(sync.dma_start(
                                out=ybq[h0 % 4][:],
                                in_=yd[32 * fb + h0:32 * fb + h0 + 1, :]
                                    .rearrange("o (p c) -> (o p) c", p=128)))
                            ld_yqp[ph] = LD.tag(sync.dma_start(
                                out=ybq[h1_ % 4][:],
                                in_=yd[32 * fb + h1_:32 * fb + h1_ + 1, :]
                                    .rearrange("o (p c) -> (o p) c", p=128)))

                        @block.vector
                        def _(vector, ph=ph, h0=h0, h1_=h1_, t_rp2=t_rp2):
                            vector.wait_ge(LD.sem, ld_yqp[ph])
                            if ph == 0:
                                vector.wait_ge(TC.sem, t_rp2)
                            if ph >= 2:
                                vector.wait_ge(TC.sem, t_mm2p[ph - 2])
                            vector.tensor_mul(xs2[0][:], ybq[h0 % 4][:],
                                              ps[3][:, 0:FBLK])
                            vector.tensor_mul(xs2[1][:], ybq[h1_ % 4][:],
                                              ps[3][:, 0:FBLK])
                            vector.tensor_copy(k2i[0][:], xs2[0][:])
                            vector.tensor_copy(k2i[1][:], xs2[1][:])
                            vector.tensor_copy(k2f[0][:], k2i[0][:])
                            vector.tensor_copy(k2f[1][:], k2i[1][:])
                            vector.tensor_sub(c2t[0][:], xs2[0][:], k2f[0][:])
                            vector.tensor_sub(c2t[1][:], xs2[1][:], k2f[1][:])
                            vector.tensor_single_scalar(m2t[0][:], c2t[0][:],
                                                        -0.5, ALU.is_le)
                            vector.tensor_single_scalar(m2t[1][:], c2t[1][:],
                                                        -0.5, ALU.is_le)
                            vector.scalar_tensor_tensor(c2t[0][:], c2t[0][:], 0.5,
                                                        k2f[0][:], ALU.is_ge,
                                                        ALU.add)
                            vector.scalar_tensor_tensor(c2t[1][:], c2t[1][:], 0.5,
                                                        k2f[1][:], ALU.is_ge,
                                                        ALU.add)
                            vector.tensor_sub(k2b[h0 % 4][:], c2t[0][:], m2t[0][:])
                            v_k2p[ph] = VC.tagw(vector, vector.tensor_sub(
                                k2b[h1_ % 4][:], c2t[1][:], m2t[1][:]))

                        @block.tensor
                        def _(tensor, ph=ph, h0=h0, h1_=h1_):
                            tensor.wait_ge(VC.sem, v_k2p[ph])
                            for hc in (h0, h1_):
                                for h in range(2):
                                    for s in range(2):
                                        mm = tensor.matmul(
                                            ps[4 + 2 * h + s][:, 0:512],
                                            k2b[hc % 4][:, 128 * h:128 * (h + 1)],
                                            t2T[hc][:, 512 * s:512 * (s + 1)],
                                            start=(hc == 0), stop=(hc == 31))
                            t_mm2p[ph] = TC.tagw(tensor, mm)

                    for h in range(2):
                        @block.sync
                        def _(sync, h=h, r0=r0, fb=fb):
                            sync.wait_ge(TC.sem, t_mm2p[NPH - 1])
                            if fb > 0:
                                sync.wait_ge(SG.sem, sg_x2[(fb - 1) * 2 + h])
                            if h == 1:
                                sync.wait_ge(SG.sem, sg_x2[fb * 2])
                            LD.tag(sync.dma_start(
                                out=x1b[h][:],
                                in_=x1d[r0 + 128 * h:r0 + 128 * (h + 1), :]))

                        ld_x1 = LD.n

                        @block.vector
                        def _(vector, h=h, ld_x1=ld_x1):
                            vector.wait_ge(LD.sem, ld_x1)
                            for s in range(2):
                                vector.tensor_mul(
                                    n2b[:, 512 * s:512 * (s + 1)],
                                    ps[4 + 2 * h + s][:, 0:512],
                                    gb_rep[:, 512 * s:512 * (s + 1)])
                            vector.scalar_tensor_tensor(
                                x1b[h][:, 0:512], n2b[:, 0:512],
                                st3[:, h:h + 1], x1b[h][:, 0:512],
                                ALU.mult, ALU.add)
                            VC.tagw(vector, vector.scalar_tensor_tensor(
                                x1b[h][:, 512:1024], n2b[:, 512:1024],
                                st3[:, h:h + 1], x1b[h][:, 512:1024],
                                ALU.mult, ALU.add))

                        v_x2 = VC.n

                        @block.gpsimd
                        def _(gpsimd, h=h, fb=fb, v_x2=v_x2):
                            gpsimd.wait_ge(VC.sem, v_x2)
                            sg_x2[fb * 2 + h] = SG.tag(gpsimd.dma_start(
                                out=outd[FBLK * fb + 128 * h:
                                         FBLK * fb + 128 * (h + 1), :],
                                in_=x1b[h][:]))

            @block.sync
            def _(sync):
                sync.wait_ge(SG.sem, SG.n)
                sync.wait_ge(LD.sem, LD.n)

    return nc


_NC_CACHE = None


def kernel(**inputs):
    global _NC_CACHE
    x = np.ascontiguousarray(np.asarray(inputs["L0_attn_x"], dtype=np.float32)
                             .reshape(16384, DIM))
    wqkv = np.ascontiguousarray(np.asarray(inputs["L0_attn_W_qkv"], np.float32))
    wout = np.ascontiguousarray(np.asarray(inputs["L0_attn_W_out"], np.float32))
    w1 = np.ascontiguousarray(np.asarray(inputs["L0_ffn_w1"], np.float32))
    convw = np.asarray(inputs["L0_ffn_conv_w"], np.float32)
    convb = np.asarray(inputs["L0_ffn_conv_b"], np.float32)
    w2 = np.ascontiguousarray(np.asarray(inputs["L0_ffn_w2"], np.float32))
    cvec = np.ascontiguousarray(np.asarray(inputs["L0_c"], np.float32)
                                .reshape(1, DIM))
    wmod = np.ascontiguousarray(np.asarray(inputs["L0_wmod"], np.float32))

    cw = convw[:, 0, :].reshape(32, 128, 3).transpose(1, 0, 2).reshape(128, 96)
    cb = convb.reshape(32, 128).T
    cw = np.ascontiguousarray(cw)
    cb = np.ascontiguousarray(cb)

    in_maps = []
    for j in range(8):
        t0 = 2048 * j
        xexts = np.zeros((ATT, DIM), np.float32)
        l0 = max(t0 - 128, 0)
        xexts[128 - (t0 - l0):128] = x[l0:t0]
        xexts[128:128 + 2048] = x[t0:t0 + 2048]
        xr = np.zeros((1, DIM), np.float32)
        if j < 7:
            xr[0] = x[t0 + 2048]
        xpres = np.zeros((PRE, DIM), np.float32)
        npr = max(t0 - 128, 0)
        xpres[0:npr] = x[0:npr]
        in_maps.append({
            "xext": xexts,
            "xr1": xr,
            "xpre": xpres,
            "npre": np.array([[float(npr)]], np.float32),
            "haslt": np.array([[1.0 if j > 0 else 0.0]], np.float32),
            "hasrt": np.array([[1.0 if j < 7 else 0.0]], np.float32),
            "wqkv": wqkv,
            "woutd": wout,
            "w1": w1,
            "w2": w2,
            "cvec": cvec,
            "wmod": wmod,
            "convw": cw,
            "convb": cb,
        })

    if _NC_CACHE is None:
        _NC_CACHE = build_nc()
    res = run_bass_kernel_spmd(_NC_CACHE, in_maps, core_ids=list(range(8)))
    out = np.concatenate([r["outd"] for r in res.results], axis=0)
    return out.reshape(1, 16384, DIM)
